# version 10 (final)
# baseline (speedup 1.0000x reference)
import numpy as np
import ml_dtypes
import concourse.bass as bass
import concourse.mybir as mybir
from concourse.bass_utils import run_bass_kernel_spmd

F32 = mybir.dt.float32
BF16 = mybir.dt.bfloat16
AF = mybir.ActivationFunctionType
ALU = mybir.AluOpType
AX = mybir.AxisListType
BF = ml_dtypes.bfloat16

S = 16384
D = 1024
CH = 512
NCH = 32
NSLOT = 8
NQ = NSLOT * CH
NTILE = 32
NEG = -1.0e30
EPS = 1e-5
RG = [[0, 1, 2, 3, 4, 5, 6, 7]]

ROLES = ("sync", "gpsimd", "vector", "scalar", "tensor")


def make_schedule():
    passes, chunks, sweeps, finals = [], [], [], []
    n = 0
    ch_ord = 0
    for s in range(4):
        slots = [7 - 2 * s, 6 - 2 * s]
        Cs = [4 * (sl + 1) for sl in slots]
        sw = {"s": s, "slots": slots, "Cs": Cs, "first_pass": n}
        for k in range(Cs[0]):
            ch = {"ord": ch_ord, "s": s, "k": k, "first_pass": n}
            for qi in (0, 1):
                if k >= Cs[qi]:
                    continue
                for t in range(4):
                    passes.append({
                        "n": n, "s": s, "k": k, "qi": qi, "t": t,
                        "slot": slots[qi], "C": Cs[qi], "chunk": ch_ord,
                        "masked": k >= Cs[qi] - 4, "first": k == 0,
                        "qtile": slots[qi] * 4 + t, "zslot": qi * 4 + t,
                        "last_q": k == Cs[qi] - 1,
                    })
                    n += 1
            ch["last_pass"] = n - 1
            chunks.append(ch)
            ch_ord += 1
        sw["last_pass"] = n - 1
        sweeps.append(sw)
    f = 0
    for p in list(passes):
        if p["last_q"] and p["t"] == 3:
            for t in range(4):
                finals.append({"f": f, "s": p["s"], "qi": p["qi"], "t": t,
                               "qtile": p["slot"] * 4 + t, "zslot": p["qi"] * 4 + t,
                               "after": p["n"]})
                f += 1
    assert n == 576 and ch_ord == 80 and f == 32
    return passes, chunks, sweeps, finals


PASSES, CHUNKS, SWEEPS, FINALS = make_schedule()
FINALS_AT = {}
for _f in FINALS:
    FINALS_AT.setdefault(_f["after"], []).append(_f)
CHUNK_BY_ORD = {c["ord"]: c for c in CHUNKS}
NP1 = NCH + NSLOT

NPH = 12


def dtsize(dt):
    return 2 if dt == BF16 else 4


TENSORS = [
    ("ident", [128, 128], BF16, 0, 11),
    ("ones11", [1, 1], F32, 0, 9),
    ("onesrow", [1, 128], F32, 0, 9),
    ("onescol", [128, 1], BF16, 0, 9),
    ("kiota", [128, 512], F32, 0, 5),
    ("iota8", [128, 8], F32, 0, 9),
    ("iota64", [128, 64], F32, 0, 10),
    ("tie8", [128, 8, 8], F32, 0, 9),
    ("tie4", [128, 4, 4], F32, 0, 9),
    ("tri8", [8, 8], F32, 0, 9),
    ("i8s", [8, 8], F32, 0, 9),
    ("mask8", [8, 8], F32, 0, 2),
    ("qidx_sb", [128, NTILE], F32, 0, 5),
    ("nidx_sb", [128, NTILE], F32, 0, 10),
    ("eb_b", [128, 8], F32, 0, 9),
    ("own8", [8, 1], F32, 0, 11),
    ("sel_sb", [128, 2], F32, 0, 0),
    ("bqr_sb", [1, D], F32, 0, 1),
    ("bkr_sb", [1, D], F32, 0, 1),
    ("bvr_sb", [1, D], F32, 0, 1),
    ("bq8_sb", [8, D], F32, 0, 1),
    ("bk8_sb", [8, D], F32, 0, 1),
    ("bv8_sb", [8, D], F32, 0, 1),
    ("ga_b", [128, D], F32, 0, 5),
    ("gb_b", [128, D], F32, 0, 11),
    ("ca1_col", [128, 8], F32, 0, 1),
    ("sa_col_bf", [128, 8], BF16, 0, 1),
    ("cb1_col", [128, 8], F32, 0, 6),
    ("sb_col_bf", [128, 8], BF16, 0, 6),
    ("bqpp", [128, 8], F32, 1, 3),
    ("bkpp", [128, 8], F32, 1, 3),
    ("bv_b", [128, D], F32, 1, 3),
    ("b1_b", [128, D], F32, 7, 8),
    ("b3_b", [128, D], F32, 7, 8),
    ("bg_b", [128, 8], F32, 7, 8),
    ("h8T_bf", [128, 8, 8], BF16, 0, 1),
    ("xf8T_bf", [128, 8, 8], BF16, 2, 4),
    ("x28_keep", [8, D], F32, 8, 11),
    ("sh8_keep", [8, D], F32, 8, 11),
    ("wf0", [128, 8, D], BF16, 1, 8),
    ("wf1", [128, 8, D], BF16, 1, 8),
    ("wf2", [128, 8, D], BF16, 1, 8),
    ("wg_f", [128, 8, 8], BF16, 6, 8),
    ("wg_raw", [128, 8, 8], BF16, 6, 6),
    ("a_tab", [64, 512], BF16, 4, 10),
    ("b_tab", [64, 512], BF16, 4, 10),
    ("logits_sb", [128, NTILE, 8], F32, 8, 9),
    ("e1_sb", [128, NTILE], F32, 9, 10),
    ("e2_sb", [128, NTILE], F32, 9, 10),
    ("wt1_sb", [128, NTILE], F32, 9, 10),
    ("wt2_sb", [128, NTILE], F32, 9, 10),
    ("offs_b", [128, 8], F32, 9, 10),
    ("wstage", [128, 2, D], F32, 1, 6),
    ("raw_bf", [128, 2, D], BF16, 1, 6),
    ("cTs", [128, 8, 2], F32, 0, 0),
    ("silu_own", [128, 8], F32, 0, 0),
    ("silu_b0", [128, 8], F32, 0, 0),
    ("wm_bf", [128, 2, 8, 512], F32, 0, 0),
    ("mod_own", [1, 6 * D], F32, 0, 0),
    ("mod_b0", [1, 6 * D], F32, 0, 0),
    ("row_t", [1, D], F32, 0, 0),
    ("row_t2", [1, D], F32, 0, 0),
    ("ca0_b", [8, D], F32, 0, 2),
    ("sa0_b", [8, D], F32, 0, 2),
    ("cb0_b", [8, D], F32, 0, 2),
    ("sb0_b", [8, D], F32, 0, 2),
    ("ga0_b", [8, D], F32, 0, 2),
    ("x8_sb", [8, D], F32, 0, 2),
    ("sq8", [8, D], BF16, 0, 2),
    ("st8", [8, 16], F32, 0, 2),
    ("h8_bf", [8, D], BF16, 0, 1),
    ("tmp8", [8, D], F32, 0, 2),
    ("bias_rows", [1, 3, D], F32, 1, 1),
    ("q8_bf", [8, D], BF16, 1, 2),
    ("k8_bf", [8, D], BF16, 1, 2),
    ("v8_bf", [8, D], BF16, 1, 2),
    ("q8T_bf", [128, 8, 8], BF16, 2, 2),
    ("k8T_bf", [128, 8, 8], BF16, 2, 2),
    ("p8T_bf", [8, 8], BF16, 2, 2),
    ("s8_sb", [8, 8], F32, 2, 2),
    ("x2_8sb", [8, D], F32, 2, 2),
    ("ln8", [8, D], F32, 0, 2),
    ("xf8_bf", [8, D], BF16, 2, 2),
    ("xin", [128, 2, 4, D], F32, 3, 3),
    ("hb", [128, 2, 4, D], BF16, 3, 3),
    ("hT_sb", [128, 2, 8, 512], BF16, 3, 3),
    ("kout", [128, 2, 8, 512], BF16, 3, 3),
    ("vout", [128, 2, 4, D], BF16, 3, 3),
    ("stats", [128, 2, 32], F32, 3, 3),
    ("sqscr", [128, D], BF16, 3, 3),
    ("abuf", [128, 2, 8, 512], BF16, 4, 4),
    ("astage", [8, 2, 512], BF16, 4, 4),
    ("kt", [128, 2, 8, 512], BF16, 5, 5),
    ("vt", [128, 2, 4, D], BF16, 5, 5),
    ("qt", [128, 2, 8, D], BF16, 5, 5),
    ("Z_sb", [128, 8, D], F32, 5, 5),
    ("P_bf", [128, 2, 512], BF16, 5, 5),
    ("PT_sb", [128, 2, 4, 128], BF16, 5, 5),
    ("xqt", [128, 2, D], F32, 5, 5),
    ("x2sb", [128, 2, D], F32, 5, 5),
    ("attst", [128, 32], F32, 5, 5),
    ("mask_sb", [128, 512], F32, 5, 5),
    ("t1sb", [128, D], F32, 6, 8),
    ("outsb", [128, 2, D], F32, 6, 8),
    ("x2f", [128, 2, D], F32, 8, 8),
    ("lnb", [128, 2, D], BF16, 8, 8),
    ("lnT", [128, 2, 8, 128], BF16, 8, 8),
    ("t3bf", [128, D], BF16, 8, 8),
    ("sgl", [128, D], BF16, 8, 8),
    ("prod", [128, D], BF16, 8, 8),
    ("prodT", [128, 2, 8, 128], BF16, 8, 8),
    ("fstats", [128, 2, 32], F32, 8, 8),
    ("fsq", [128, D], BF16, 8, 8),
    ("scor", [128, NTILE, 8], F32, 9, 9),
    ("sbias", [128, NTILE, 8], F32, 9, 9),
    ("cmp", [128, NTILE, 8, 8], F32, 9, 9),
    ("cmp2", [128, NTILE, 8, 8], F32, 9, 9),
    ("rank8", [128, NTILE, 8], F32, 9, 9),
    ("grp", [128, NTILE, 4], F32, 9, 9),
    ("rank4", [128, NTILE, 4], F32, 9, 9),
    ("keep4", [128, NTILE, 4], F32, 9, 9),
    ("keep8", [128, NTILE, 8], F32, 9, 9),
    ("maskt", [128, NTILE, 8], F32, 9, 9),
    ("oh1", [128, NTILE, 8], F32, 9, 9),
    ("oh2", [128, NTILE, 8], F32, 9, 9),
    ("ohsum", [128, NTILE * 8], BF16, 9, 9),
    ("mx_r", [128, NTILE], F32, 9, 9),
    ("cnt_row", [1, 256], F32, 9, 9),
    ("cnt8", [1, 8], F32, 9, 9),
    ("cnt_col", [8, 1], F32, 9, 9),
    ("offs_col", [8, 1], F32, 9, 9),
    ("offs_row", [1, 8], F32, 9, 9),
    ("ohT", [128, 2, 64], BF16, 10, 10),
    ("ohs", [64, 2, 128], BF16, 10, 10),
    ("term", [128, 2, 512], BF16, 10, 10),
    ("siluh", [128, 2, 512], F32, 10, 10),
    ("jj64", [128, 64], F32, 10, 10),
    ("e64", [128, 64], F32, 10, 10),
    ("w64", [128, 64], F32, 10, 10),
    ("row10", [128, 64], F32, 10, 10),
    ("cmpj", [128, 64, 8], F32, 10, 10),
    ("h_f", [64, 512], F32, 10, 11),
    ("w2buf", [128, 2, 4, D], BF16, 10, 11),
    ("h_bf", [64, 512], BF16, 11, 11),
    ("HT_sb", [128, 4, 64], BF16, 11, 11),
    ("r8", [8, D], F32, 11, 11),
    ("out8", [8, D], F32, 11, 11),
]


def layout_sbuf(base, top):
    placed = []
    offsets = {}
    items = sorted(TENSORS, key=lambda t: -int(np.prod(t[1][1:])) * dtsize(t[2]))
    for name, shape, dt, fp, lp in items:
        sz = int(np.prod(shape[1:])) * dtsize(dt)
        sz = (sz + 31) // 32 * 32
        cands = sorted({base} | {e for (o, e, f, l) in placed
                                 if not (l < fp or f > lp)})
        for off in cands:
            off = (off + 31) // 32 * 32
            if off + sz > top:
                continue
            ok = True
            for (o, e, f, l) in placed:
                if l < fp or f > lp:
                    continue
                if not (off + sz <= o or off >= e):
                    ok = False
                    break
            if ok:
                placed.append((off, off + sz, fp, lp))
                offsets[name] = off
                break
        else:
            raise RuntimeError(f"SBUF layout failed for {name}")
    return offsets, max(e for (o, e, f, l) in placed)


class KB:
    def __init__(self):
        self.mode = "plan"
        self.sems = {}
        self.events = {}
        self.handles = {}
        self.emit_sems = {}

    def reset_emit(self):
        self.mode = "emit"
        self.emit_sems = {k: 0 for k in self.sems}


class E:

    def __init__(self, kb, role, real):
        self.__dict__["kb"] = kb
        self.__dict__["role"] = role
        self.__dict__["real"] = real
        self.__dict__["last"] = None
        self.__dict__["state"] = "fenced"
        self.__dict__["last_inc"] = None

    def __getattr__(self, name):
        if self.kb.mode == "plan":
            def planwrap(*a, **k):
                self.__dict__["state"] = "op"
                return None
            if name in ("wait_ge", "sem_inc"):
                return lambda *a, **k: None
            return planwrap
        fn = getattr(self.real, name)

        def wrap(*a, **k):
            r = fn(*a, **k)
            if name not in ("wait_ge", "sem_inc"):
                self.__dict__["last"] = r
                self.__dict__["state"] = "op"
            return r
        return wrap

    def inc(self, semname, label=None, k=1):
        kb = self.kb
        if kb.mode == "plan":
            kb.sems[semname] = kb.sems.get(semname, 0) + k
            if label is not None:
                assert label not in kb.events, f"dup label {label}"
                kb.events[label] = (semname, kb.sems[semname])
            if self.state == "op":
                self.__dict__["state"] = "op+inc"
            return
        kb.emit_sems[semname] = kb.emit_sems.get(semname, 0) + k
        if self.state == "op":
            self.last.then_inc(kb.handles[semname], k)
            self.__dict__["state"] = "op+inc"
            self.__dict__["last_inc"] = (semname, kb.emit_sems[semname])
        elif self.state == "fenced":
            self.real.sem_inc(kb.handles[semname], k)
        else:
            raise RuntimeError(f"double inc on one instruction ({semname})")

    def fence(self):
        kb = self.kb
        fr = f"fz_{self.role}"
        if kb.mode == "plan":
            if self.state == "op":
                kb.sems[fr] = kb.sems.get(fr, 0) + 1
                self.__dict__["state"] = "fenced"
            elif self.state == "op+inc":
                self.__dict__["state"] = "fenced"
            return
        if self.state == "op":
            kb.emit_sems[fr] = kb.emit_sems.get(fr, 0) + 1
            self.last.then_inc(kb.handles[fr], 1)
            self.real.wait_ge(kb.handles[fr], kb.emit_sems[fr])
            self.__dict__["state"] = "fenced"
        elif self.state == "op+inc":
            semname, val = self.last_inc
            self.real.wait_ge(kb.handles[semname], val)
            self.__dict__["state"] = "fenced"

    def wait(self, label, missing_ok=False):
        kb = self.kb
        if kb.mode == "plan":
            return
        ev = kb.events.get(label)
        if ev is None:
            if missing_ok:
                return
            raise KeyError(f"unknown event {label}")
        semname, val = ev
        self.real.wait_ge(kb.handles[semname], val)

    def wait_total(self, semname):
        kb = self.kb
        if kb.mode == "plan":
            return
        self.real.wait_ge(kb.handles[semname], kb.sems[semname])


def ln_stats_vector(e, st):
    e.fence()
    e.tensor_scalar(st(1), st(1), 1.0 / D, None, ALU.mult)
    e.fence()
    e.tensor_tensor(st(2), st(1), st(1), ALU.mult)
    e.fence()
    e.scalar_tensor_tensor(st(2), st(0), 1.0 / D, st(2), ALU.mult, ALU.subtract)
    e.fence()
    e.tensor_scalar(st(2), st(2), EPS, None, ALU.add)
    e.fence()


def ln_polish_vector(e, st):
    e.reciprocal(st(3), st(5))
    e.fence()
    e.tensor_tensor(st(6), st(3), st(3), ALU.mult)
    e.fence()
    e.tensor_tensor(st(6), st(6), st(2), ALU.mult)
    e.fence()
    e.tensor_scalar(st(6), st(6), -0.5, 1.5, ALU.mult, ALU.add)
    e.fence()
    e.tensor_tensor(st(3), st(3), st(6), ALU.mult)
    e.fence()
    e.tensor_tensor(st(4), st(1), st(3), ALU.mult)
    e.fence()
    e.tensor_scalar(st(4), st(4), -1.0, None, ALU.mult)
    e.fence()


def clamp_row_vector(e, src, dst, tmp):
    e.tensor_single_scalar(dst, src, 0.0, ALU.is_ge)
    e.tensor_scalar(dst, dst, 2.0, 1.0, ALU.mult, ALU.subtract)
    e.tensor_scalar(tmp, src, -1.0, None, ALU.mult)
    e.tensor_tensor(tmp, tmp, src, ALU.max)
    e.tensor_scalar(tmp, tmp, 0.25, None, ALU.max)
    e.tensor_tensor(dst, dst, tmp, ALU.mult)


NWG = 12



def ph0(ctx, kb, role, e):
    T = ctx["T"]
    dr = ctx["dr"]
    ps_s0, ps_s1, ps_p0, ps_p1, ps_z0, ps_z1 = ctx["psum"]
    consts = [
        ("ident", "ident_c"), ("ones11", "ones11_c"), ("onesrow", "onesrow_c"),
        ("onescol", "onescol_c"), ("kiota", "kiota_c"), ("iota8", "iota8_c"),
        ("iota64", "iota64_c"), ("tie8", "tie8_c"), ("tie4", "tie4_c"),
        ("tri8", "tri8_c"), ("i8s", "i8_c"), ("mask8", "mask8_c"),
        ("qidx_sb", "qidx"), ("nidx_sb", "nidx"), ("eb_b", "eb_in"),
        ("own8", "own_in"), ("sel_sb", "sel_in"),
        ("bqr_sb", "bq_r"), ("bkr_sb", "bk_r"), ("bvr_sb", "bv_r"),
        ("bq8_sb", "bq8"), ("bk8_sb", "bk8"), ("bv8_sb", "bv8"),
        ("x8_sb", "x8"),
    ]

    def sl(row, j):
        return row[0:1, j * D:(j + 1) * D]

    if role == "sync":
        for nm, srcname in consts:
            ap = dr[srcname]
            nd = len(ap.shape)
            e.dma_start(out=T[nm][...], in_=ap[(slice(None),) * nd])
            e.inc("d_c0", k=16)
        e.dma_start(out=T["cTs"][...],
                    in_=dr["cT"].rearrange("(c p) b -> p c b", p=128))
        e.inc("d_c0", "c0.done", 16)

    elif role == "gpsimd":
        for g in range(NWG):
            if g >= 2:
                e.wait(f"p0.modg{g-2}")
            e.dma_start(
                out=T["wm_bf"][:, g % 2, :, :],
                in_=dr["wmod"].rearrange("(c p) n -> p c n", p=128)[:, :, g * 512:(g + 1) * 512])
            e.inc(f"d_wm{g % 2}", f"p0.wm{g}", 16)

    elif role == "vector":
        e.wait("c0.done")
        tmpc = T["ca1_col"]
        e.tensor_scalar(tmpc[:, :], T["cTs"][:, :, 0], T["sel_sb"][:, 0:1], None,
                        ALU.mult)
        e.fence()
        e.scalar_tensor_tensor(tmpc[:, :], T["cTs"][:, :, 1], T["sel_sb"][:, 1:2],
                               tmpc[:, :], ALU.mult, ALU.add)
        e.inc("v_a", "p0.ownc")
        for g in range(NWG):
            e.wait(f"p0.modg{g}")
            e.tensor_copy(T["mod_own"][0:1, g * 512:(g + 1) * 512], ps_z0[0:1, 0:512])
            e.tensor_copy(T["mod_b0"][0:1, g * 512:(g + 1) * 512], ps_z0[0:1, 512:1024])
            e.inc("v_a", f"p0.modev{g}")
        mo, mb = T["mod_own"], T["mod_b0"]
        clamp_row_vector(e, sl(mo, 2), T["row_t"][:], T["row_t2"][:])
        e.inc("v_a", "p0.rga")
        e.wait("p0.bga")
        clamp_row_vector(e, sl(mo, 5), T["row_t"][:], T["row_t2"][:])
        e.inc("v_a", "p0.rgb")
        e.wait("p0.bgb")
        clamp_row_vector(e, sl(mb, 2), T["row_t"][:], T["row_t2"][:])
        e.inc("v_a", "p0.rga0")
        e.wait("p0.bga0")
        e.tensor_scalar(T["row_t"][:], sl(mo, 1), 1.0, None, ALU.add)
        e.inc("v_a", "p0.rca1")
        e.wait("p0.bca1")
        e.tensor_scalar(T["row_t"][:], sl(mo, 4), 1.0, None, ALU.add)
        e.inc("v_a", "p0.rcb1")
        st = lambda i: T["st8"][0:8, i:i + 1]
        e.reduce_sum(st(1), T["x8_sb"][0:8, :], AX.X)
        e.wait("p0.sq8")
        e.reduce_sum(st(0), T["tmp8"][0:8, :], AX.X)
        ln_stats_vector(e, st)
        e.inc("v_a", "p0.ln8a")
        e.wait("p0.sqrt8")
        ln_polish_vector(e, st)
        e.inc("v_a", "p0.ln8r")
        e.wait("p0.ln8ap")
        e.wait("p0.eb0r2")
        e.tensor_tensor(T["tmp8"][0:8, :], T["ln8"][0:8, :], T["ca0_b"][0:8, :], ALU.mult)
        e.tensor_tensor(T["tmp8"][0:8, :], T["tmp8"][0:8, :], T["ln8"][0:8, :], ALU.add)
        e.tensor_tensor(T["h8_bf"][0:8, :], T["tmp8"][0:8, :], T["sa0_b"][0:8, :], ALU.add)
        e.inc("v_a", "p0.h8")

    elif role == "scalar":
        e.wait("p0.ownc")
        e.activation(T["silu_own"][:, :], T["ca1_col"][:, :], AF.Silu)
        e.activation(T["silu_b0"][:, :], T["cTs"][:, :, 0], AF.Silu)
        e.inc("s_a", "p0.siluc")
        e.wait("p0.bga")
        e.copy(T["ga_b"][:, :], ps_z1[:, :])
        e.inc("s_a", "p0.ega")
        e.wait("p0.bgb")
        e.copy(T["gb_b"][:, :], ps_z1[:, :])
        e.inc("s_a", "p0.egb")
        e.wait("p0.bga0")
        e.copy(T["ga0_b"][0:8, :], ps_z0[0:8, :])
        e.inc("s_a", "p0.ega0")
        e.wait("p0.bca1")
        e.copy(T["ca1_col"][:, :], ps_p0[:, 0:8])
        e.inc("s_a", "p0.eca1")
        e.wait("p0.bcb1")
        e.copy(T["cb1_col"][:, :], ps_p1[:, 0:8])
        e.inc("s_a", "p0.ecb1")
        e.wait("p0.bsa")
        e.copy(T["sa_col_bf"][:, :], ps_s0[:, 0:8])
        e.inc("s_a", "p0.esa")
        e.wait("p0.bsb")
        e.copy(T["sb_col_bf"][:, :], ps_s1[:, 0:8])
        e.inc("s_a", "p0.esb")
        e.wait("p0.b0r1")
        e.copy(T["ca0_b"][0:8, 0:512], ps_s0[0:8, 0:512])
        e.copy(T["sa0_b"][0:8, 0:512], ps_s1[0:8, 0:512])
        e.copy(T["cb0_b"][0:8, 0:512], ps_p0[0:8, 0:512])
        e.copy(T["sb0_b"][0:8, 0:512], ps_p1[0:8, 0:512])
        e.inc("s_a", "p0.eb0r1")
        e.wait("p0.b0r2")
        e.copy(T["ca0_b"][0:8, 512:1024], ps_s0[0:8, 0:512])
        e.copy(T["sa0_b"][0:8, 512:1024], ps_s1[0:8, 0:512])
        e.copy(T["cb0_b"][0:8, 512:1024], ps_p0[0:8, 0:512])
        e.copy(T["sb0_b"][0:8, 512:1024], ps_p1[0:8, 0:512])
        e.inc("s_a", "p0.eb0r2")
        e.activation(T["tmp8"][0:8, :], T["x8_sb"][0:8, :], AF.Square)
        e.inc("s_a", "p0.sq8")
        e.wait("p0.ln8a")
        e.activation(T["st8"][0:8, 5:6], T["st8"][0:8, 2:3], AF.Sqrt)
        e.inc("s_a", "p0.sqrt8")
        e.wait("p0.ln8r")
        e.activation(T["ln8"][0:8, :], T["x8_sb"][0:8, :], AF.Identity,
                     bias=T["st8"][0:8, 4:5], scale=T["st8"][0:8, 3:4])
        e.inc("s_a", "p0.ln8ap")
        e.wait("p0.th8T")
        e.copy(T["h8T_bf"][:, :, :], ps_p0[:, 0:64])
        e.inc("s_a", "p0.h8Tev")

    elif role == "tensor":
        e.wait("p0.siluc")
        for g in range(NWG):
            e.wait(f"p0.wm{g}")
            if g >= 1:
                e.wait(f"p0.modev{g-1}")
            for c in range(8):
                e.matmul(ps_z0[0:1, 0:512], T["silu_own"][:, c:c + 1],
                         T["wm_bf"][:, g % 2, c, :], start=(c == 0), stop=(c == 7))
            for c in range(8):
                e.matmul(ps_z0[0:1, 512:1024], T["silu_b0"][:, c:c + 1],
                         T["wm_bf"][:, g % 2, c, :], start=(c == 0), stop=(c == 7))
            e.inc("t_a", f"p0.modg{g}")
        mo, mb = T["mod_own"], T["mod_b0"]
        e.wait("p0.rga")
        e.matmul(ps_z1[:, 0:512], T["onesrow"][0:1, :], T["row_t"][0:1, 0:512],
                 start=True, stop=True)
        e.matmul(ps_z1[:, 512:1024], T["onesrow"][0:1, :], T["row_t"][0:1, 512:1024],
                 start=True, stop=True)
        e.inc("t_a", "p0.bga")
        e.wait("p0.rgb")
        e.wait("p0.ega")
        e.matmul(ps_z1[:, 0:512], T["onesrow"][0:1, :], T["row_t"][0:1, 0:512],
                 start=True, stop=True)
        e.matmul(ps_z1[:, 512:1024], T["onesrow"][0:1, :], T["row_t"][0:1, 512:1024],
                 start=True, stop=True)
        e.inc("t_a", "p0.bgb")
        e.wait("p0.rga0")
        e.wait("p0.egb")
        e.matmul(ps_z0[0:8, 0:512], T["onesrow"][0:1, 0:8], T["row_t"][0:1, 0:512],
                 start=True, stop=True)
        e.matmul(ps_z0[0:8, 512:1024], T["onesrow"][0:1, 0:8],
                 T["row_t"][0:1, 512:1024], start=True, stop=True)
        e.inc("t_a", "p0.bga0")
        e.wait("p0.rca1")
        e.wait("p0.ega0")
        for c in range(8):
            e.matmul(ps_p0[:, c:c + 1], T["row_t"][0:1, c * 128:(c + 1) * 128],
                     T["ones11"][0:1, 0:1], start=True, stop=True)
        e.inc("t_a", "p0.bca1")
        e.wait("p0.rcb1")
        for c in range(8):
            e.matmul(ps_p1[:, c:c + 1], T["row_t"][0:1, c * 128:(c + 1) * 128],
                     T["ones11"][0:1, 0:1], start=True, stop=True)
        e.inc("t_a", "p0.bcb1")
        for c in range(8):
            e.matmul(ps_s0[:, c:c + 1], mo[0:1, c * 128:(c + 1) * 128],
                     T["ones11"][0:1, 0:1], start=True, stop=True)
        e.inc("t_a", "p0.bsa")
        for c in range(8):
            e.matmul(ps_s1[:, c:c + 1], mo[0:1, 3 * D + c * 128:3 * D + (c + 1) * 128],
                     T["ones11"][0:1, 0:1], start=True, stop=True)
        e.inc("t_a", "p0.bsb")
        e.wait("p0.esa")
        e.wait("p0.esb")
        e.wait("p0.eca1")
        e.wait("p0.ecb1")
        for (idx, j) in enumerate([1, 0, 4, 3]):
            dst = [ps_s0, ps_s1, ps_p0, ps_p1][idx]
            e.matmul(dst[0:8, 0:512], T["onesrow"][0:1, 0:8],
                     sl(mb, j)[0:1, 0:512], start=True, stop=True)
        e.inc("t_a", "p0.b0r1")
        e.wait("p0.eb0r1")
        for (idx, j) in enumerate([1, 0, 4, 3]):
            dst = [ps_s0, ps_s1, ps_p0, ps_p1][idx]
            e.matmul(dst[0:8, 0:512], T["onesrow"][0:1, 0:8],
                     sl(mb, j)[0:1, 512:1024], start=True, stop=True)
        e.inc("t_a", "p0.b0r2")
        e.wait("p0.h8")
        e.wait("p0.eb0r2")
        for c in range(8):
            e.matmul(ps_p0[:, c * 8:(c + 1) * 8], T["h8_bf"][0:8, c * 128:(c + 1) * 128],
                     T["ident"][0:8, 0:8], start=True, stop=True)
        e.inc("t_a", "p0.th8T")



def ph1(ctx, kb, role, e):
    T = ctx["T"]
    dr = ctx["dr"]
    ps_s0, ps_s1, ps_p0, ps_p1, ps_z0, ps_z1 = ctx["psum"]
    WSRC = ["wqT", "wkT", "wvT"]
    WDST = ["wf0", "wf1", "wf2"]
    BIAS = ["bqr_sb", "bkr_sb", "bvr_sb"]

    if role == "sync":
        for w in range(3):
            for c in range(8):
                ci = w * 8 + c
                if ci >= 2:
                    e.wait(f"p1.fold{ci-2}")
                e.dma_start(out=T["wstage"][:, ci % 2, :],
                            in_=dr[WSRC[w]].rearrange("(c p) n -> p c n", p=128)[:, c, :])
                e.inc(f"d_ws{ci % 2}", f"p1.ws{ci}", 16)

    elif role == "vector":
        for w in range(3):
            for c in range(8):
                ci = w * 8 + c
                e.wait(f"p1.ws{ci}")
                if w == 0:
                    e.tensor_scalar(T[WDST[w]][:, c, :], T["wstage"][:, ci % 2, :],
                                    T["ca1_col"][:, c:c + 1], 1.0 / 32.0,
                                    ALU.mult, ALU.mult)
                else:
                    e.tensor_scalar(T[WDST[w]][:, c, :], T["wstage"][:, ci % 2, :],
                                    T["ca1_col"][:, c:c + 1], None, ALU.mult)
                e.wait(f"p1.mm{ci-2}", missing_ok=True)
                e.tensor_copy(T["raw_bf"][:, ci % 2, :], T["wstage"][:, ci % 2, :])
                e.inc("v_a", f"p1.fold{ci}")
            e.wait(f"p1.mm{w*8+7}")
            e.tensor_tensor(T["bias_rows"][0:1, w, :], ps_z0[0:1, :],
                            T[BIAS[w]][0:1, :], ALU.add)
            if w == 0:
                e.tensor_scalar(T["bias_rows"][0:1, w, :], T["bias_rows"][0:1, w, :],
                                1.0 / 32.0, None, ALU.mult)
            e.inc("v_a", f"p1.bias{w}")
        e.scalar_tensor_tensor(T["q8_bf"][0:8, :], ps_z1[0:8, :], 0.0,
                               T["bq8_sb"][0:8, :], ALU.add, ALU.add)
        e.scalar_tensor_tensor(T["k8_bf"][0:8, 0:512], ps_s0[0:8, :], 0.0,
                               T["bk8_sb"][0:8, 0:512], ALU.add, ALU.add)
        e.scalar_tensor_tensor(T["k8_bf"][0:8, 512:1024], ps_s1[0:8, :], 0.0,
                               T["bk8_sb"][0:8, 512:1024], ALU.add, ALU.add)
        e.scalar_tensor_tensor(T["v8_bf"][0:8, 0:512], ps_p0[0:8, :], 0.0,
                               T["bv8_sb"][0:8, 0:512], ALU.add, ALU.add)
        e.scalar_tensor_tensor(T["v8_bf"][0:8, 512:1024], ps_p1[0:8, :], 0.0,
                               T["bv8_sb"][0:8, 512:1024], ALU.add, ALU.add)
        e.inc("v_a", "p1.mini")

    elif role == "tensor":
        for w in range(3):
            for c in range(8):
                ci = w * 8 + c
                e.wait(f"p1.fold{ci}")
                if c == 0 and w > 0:
                    e.wait(f"p1.bias{w-1}")
                e.matmul(ps_z0[0:1, 0:512], T["sa_col_bf"][:, c:c + 1],
                         T["raw_bf"][:, ci % 2, 0:512], start=(c == 0), stop=(c == 7))
                e.matmul(ps_z0[0:1, 512:1024], T["sa_col_bf"][:, c:c + 1],
                         T["raw_bf"][:, ci % 2, 512:1024], start=(c == 0), stop=(c == 7))
                lhs = T["h8T_bf"][:, c, :]
                if w == 0:
                    e.matmul(ps_z1[0:8, 0:512], lhs, T["raw_bf"][:, ci % 2, 0:512],
                             start=(c == 0), stop=(c == 7))
                    e.matmul(ps_z1[0:8, 512:1024], lhs, T["raw_bf"][:, ci % 2, 512:1024],
                             start=(c == 0), stop=(c == 7))
                elif w == 1:
                    e.matmul(ps_s0[0:8, 0:512], lhs, T["raw_bf"][:, ci % 2, 0:512],
                             start=(c == 0), stop=(c == 7))
                    e.matmul(ps_s1[0:8, 0:512], lhs, T["raw_bf"][:, ci % 2, 512:1024],
                             start=(c == 0), stop=(c == 7))
                else:
                    e.matmul(ps_p0[0:8, 0:512], lhs, T["raw_bf"][:, ci % 2, 0:512],
                             start=(c == 0), stop=(c == 7))
                    e.matmul(ps_p1[0:8, 0:512], lhs, T["raw_bf"][:, ci % 2, 512:1024],
                             start=(c == 0), stop=(c == 7))
                e.inc("t_a", f"p1.mm{ci}")
        e.wait("p1.mini")
        e.wait("p1.bias2")
        for w in range(2):
            dst = ps_z0 if w == 0 else ps_z1
            for c in range(8):
                e.matmul(dst[:, c:c + 1], T["bias_rows"][0:1, w, c * 128:(c + 1) * 128],
                         T["ones11"][0:1, 0:1], start=True, stop=True)
            e.inc("t_a", f"p1.bcol{w}")
        e.matmul(ps_s0[:, 0:512], T["onesrow"][0:1, :], T["bias_rows"][0:1, 2, 0:512],
                 start=True, stop=True)
        e.matmul(ps_s1[:, 0:512], T["onesrow"][0:1, :], T["bias_rows"][0:1, 2, 512:1024],
                 start=True, stop=True)
        e.inc("t_a", "p1.bvb")

    elif role == "scalar":
        e.wait("p1.bcol0")
        e.copy(T["bqpp"][:, :], ps_z0[:, 0:8])
        e.inc("s_a", "p1.ebq")
        e.wait("p1.bcol1")
        e.copy(T["bkpp"][:, :], ps_z1[:, 0:8])
        e.inc("s_a", "p1.ebk")
        e.wait("p1.bvb")
        e.copy(T["bv_b"][:, 0:512], ps_s0[:, :])
        e.copy(T["bv_b"][:, 512:1024], ps_s1[:, :])
        e.inc("s_a", "p1.ebv")



def ph2(ctx, kb, role, e):
    T = ctx["T"]
    ps_s0, ps_s1, ps_p0, ps_p1, ps_z0, ps_z1 = ctx["psum"]
    st = lambda i: T["st8"][0:8, i:i + 1]

    if role == "tensor":
        for c in range(8):
            e.matmul(ps_p0[:, c * 8:(c + 1) * 8], T["q8_bf"][0:8, c * 128:(c + 1) * 128],
                     T["ident"][0:8, 0:8], start=True, stop=True)
        for c in range(8):
            e.matmul(ps_p1[:, c * 8:(c + 1) * 8], T["k8_bf"][0:8, c * 128:(c + 1) * 128],
                     T["ident"][0:8, 0:8], start=True, stop=True)
        e.inc("t_a", "p2.qkT")
        e.wait("p2.qkTev")
        for c in range(8):
            e.matmul(ps_s0[0:8, 0:8], T["q8T_bf"][:, c, :], T["k8T_bf"][:, c, :],
                     start=(c == 0), stop=(c == 7))
        e.inc("t_a", "p2.s8")
        e.wait("p2.p8")
        e.matmul(ps_p0[0:8, 0:8], T["p8T_bf"][0:8, 0:8], T["ident"][0:8, 0:8],
                 start=True, stop=True)
        e.inc("t_a", "p2.p8T")
        e.wait("p2.p8Tev")
        e.matmul(ps_z0[0:8, 0:512], T["p8T_bf"][0:8, 0:8], T["v8_bf"][0:8, 0:512],
                 start=True, stop=True)
        e.matmul(ps_z0[0:8, 512:1024], T["p8T_bf"][0:8, 0:8], T["v8_bf"][0:8, 512:1024],
                 start=True, stop=True)
        e.inc("t_a", "p2.z8")
        e.wait("p2.xf8")
        for c in range(8):
            e.matmul(ps_p0[:, c * 8:(c + 1) * 8], T["xf8_bf"][0:8, c * 128:(c + 1) * 128],
                     T["ident"][0:8, 0:8], start=True, stop=True)
        e.inc("t_a", "p2.xf8T")

    elif role == "scalar":
        e.wait("p2.qkT")
        e.copy(T["q8T_bf"][:, :, :], ps_p0[:, 0:64])
        e.copy(T["k8T_bf"][:, :, :], ps_p1[:, 0:64])
        e.inc("s_a", "p2.qkTev")
        e.wait("p2.s8m")
        e.activation(T["p8T_bf"][0:8, 0:8], T["s8_sb"][0:8, 0:8], AF.Exp,
                     bias=st(6))
        e.inc("s_a", "p2.p8")
        e.wait("p2.p8T")
        e.wait("p2.den")
        e.copy(T["p8T_bf"][0:8, 0:8], ps_p0[0:8, 0:8])
        e.inc("s_a", "p2.p8Tev")
        e.wait("p2.x28")
        e.activation(T["tmp8"][0:8, :], T["x2_8sb"][0:8, :], AF.Square)
        e.inc("s_a", "p2.sq28")
        e.wait("p2.lnva")
        e.activation(st(5), st(2), AF.Sqrt)
        e.inc("s_a", "p2.sqrt28")
        e.wait("p2.lnvr")
        e.activation(T["ln8"][0:8, :], T["x2_8sb"][0:8, :], AF.Identity,
                     bias=st(4), scale=st(3))
        e.inc("s_a", "p2.ln28")
        e.wait("p2.xf8T")
        e.copy(T["xf8T_bf"][:, :, :], ps_p0[:, 0:64])
        e.inc("s_a", "p2.xf8Tev")

    elif role == "vector":
        e.wait("p2.s8")
        e.scalar_tensor_tensor(T["s8_sb"][0:8, 0:8], ps_s0[0:8, 0:8], 1.0 / 32.0,
                               T["mask8"][0:8, 0:8], ALU.mult, ALU.add)
        e.fence()
        e.reduce_max(st(6), T["s8_sb"][0:8, 0:8], AX.X, negate=True)
        e.fence()
        e.inc("v_a", "p2.s8m")
        e.wait("p2.p8")
        e.reduce_sum(st(7), T["p8T_bf"][0:8, 0:8], AX.X)
        e.fence()
        e.inc("v_a", "p2.den")
        e.wait("p2.z8")
        e.reciprocal(st(8), st(7))
        e.fence()
        e.tensor_scalar(T["tmp8"][0:8, :], ps_z0[0:8, :], st(8), None, ALU.mult)
        e.tensor_tensor(T["tmp8"][0:8, :], T["tmp8"][0:8, :], T["ga0_b"][0:8, :],
                        ALU.mult)
        e.tensor_tensor(T["x2_8sb"][0:8, :], T["tmp8"][0:8, :], T["x8_sb"][0:8, :],
                        ALU.add)
        e.inc("v_a", "p2.x28")
        e.reduce_sum(st(1), T["x2_8sb"][0:8, :], AX.X)
        e.wait("p2.sq28")
        e.reduce_sum(st(0), T["tmp8"][0:8, :], AX.X)
        ln_stats_vector(e, st)
        e.inc("v_a", "p2.lnva")
        e.wait("p2.sqrt28")
        ln_polish_vector(e, st)
        e.inc("v_a", "p2.lnvr")
        e.wait("p2.ln28")
        e.tensor_tensor(T["tmp8"][0:8, :], T["ln8"][0:8, :], T["cb0_b"][0:8, :],
                        ALU.mult)
        e.tensor_tensor(T["tmp8"][0:8, :], T["tmp8"][0:8, :], T["ln8"][0:8, :],
                        ALU.add)
        e.tensor_tensor(T["xf8_bf"][0:8, :], T["tmp8"][0:8, :], T["sb0_b"][0:8, :],
                        ALU.add)
        e.inc("v_a", "p2.xf8")



def ph3(ctx, kb, role, e):
    T = ctx["T"]
    dr = ctx["dr"]
    ps_s0, ps_s1, ps_p0, ps_p1, ps_z0, ps_z1 = ctx["psum"]
    PS_S = [ps_s0, ps_s1]
    PS_P = [ps_p0, ps_p1]
    PS_Z = [ps_z0, ps_z1]
    kT3 = dr["kT_d"].rearrange("(c p) n -> p c n", p=128)
    qT3 = dr["qT_d"].rearrange("(c p) n -> p c n", p=128)

    if role == "sync":
        for i in range(NP1):
            e.wait(f"p3.ln{i-2}.3", missing_ok=True)
            if i < NCH:
                src = dr["x_kv"][i * CH:(i + 1) * CH, :]
            else:
                src = dr["x_q"][(i - NCH) * CH:(i - NCH + 1) * CH, :]
            e.dma_start(out=T["xin"][:, i % 2, :, :],
                        in_=src.rearrange("(t p) d -> p t d", p=128))
            e.inc(f"d_ldx{i % 2}", f"p3.x{i}", 16)

    elif role == "scalar":
        for i in range(NP1):
            for t in range(4):
                stt = lambda j: T["stats"][:, i % 2, t * 8 + j:t * 8 + j + 1]
                e.wait(f"p3.x{i}")
                e.activation(T["sqscr"][:, :], T["xin"][:, i % 2, t, :], AF.Square,
                             accum_out=stt(0))
                e.inc("s_a", f"p3.sq{i}.{t}")
                e.wait(f"p3.va{i}.{t}")
                e.activation(stt(5), stt(2), AF.Sqrt)
                e.inc("s_a", f"p3.sqrt{i}.{t}")
                e.wait(f"p3.vr{i}.{t}")
                e.wait(f"p3.tr{i-2}.7", missing_ok=True)
                e.activation(T["hb"][:, i % 2, t, :], T["xin"][:, i % 2, t, :],
                             AF.Identity, bias=stt(4), scale=stt(3))
                e.inc("s_a", f"p3.ln{i}.{t}")

    elif role == "vector":
        for i in range(NP1):
            is_q = i >= NCH
            for t in range(4):
                stt = lambda j: T["stats"][:, i % 2, t * 8 + j:t * 8 + j + 1]
                e.wait(f"p3.x{i}")
                e.reduce_sum(stt(1), T["xin"][:, i % 2, t, :], AX.X)
                e.wait(f"p3.sq{i}.{t}")
                ln_stats_vector(e, stt)
                e.inc("v_a", f"p3.va{i}.{t}")
                e.wait(f"p3.sqrt{i}.{t}")
                ln_polish_vector(e, stt)
                e.inc("v_a", f"p3.vr{i}.{t}")
            for c in range(8):
                e.wait(f"p3.tr{i}.{c}")
                e.tensor_copy(T["hT_sb"][:, i % 2, c, :], PS_P[c % 2][:, :])
                e.inc("v_a", f"p3.htev{i}.{c}")
            bcol = T["bqpp"] if is_q else T["bkpp"]
            for dk in range(8):
                e.wait(f"p3.k{i}.{dk}")
                e.wait(f"p3.kst{i-2}", missing_ok=True)
                e.tensor_scalar(T["kout"][:, i % 2, dk, :], PS_S[dk % 2][:, :],
                                bcol[:, dk:dk + 1], None, ALU.add)
                e.inc("v_a", f"p3.kev{i}.{dk}")
            if not is_q:
                for tt in range(4):
                    e.wait(f"p3.v{i}.{tt}")
                    e.wait(f"p3.vst{i-2}", missing_ok=True)
                    e.scalar_tensor_tensor(T["vout"][:, i % 2, tt, :],
                                           PS_Z[tt % 2][:, :], 0.0,
                                           T["bv_b"][:, :], ALU.add, ALU.add)
                    e.inc("v_a", f"p3.vev{i}.{tt}")

    elif role == "tensor":
        for i in range(NP1):
            is_q = i >= NCH
            for c in range(8):
                e.wait(f"p3.ln{i}.3")
                e.wait(f"p3.htev{i}.{c-2}" if c >= 2 else f"p3.htev{i-1}.{c+6}",
                       missing_ok=True)
                for t in range(4):
                    e.matmul(PS_P[c % 2][:, t * 128:(t + 1) * 128],
                             T["hb"][:, i % 2, t, c * 128:(c + 1) * 128],
                             T["ident"][:, :], start=True, stop=True)
                e.inc("t_a", f"p3.tr{i}.{c}")
            wsrc = T["wf0"] if is_q else T["wf1"]
            for dk in range(8):
                e.wait(f"p3.htev{i}.7")
                e.wait(f"p3.kev{i}.{dk-2}" if dk >= 2 else f"p3.kev{i-1}.{dk+6}",
                       missing_ok=True)
                for dc in range(8):
                    e.matmul(PS_S[dk % 2][:, :], wsrc[:, dc, dk * 128:(dk + 1) * 128],
                             T["hT_sb"][:, i % 2, dc, :],
                             start=(dc == 0), stop=(dc == 7))
                e.inc("t_a", f"p3.k{i}.{dk}")
            if not is_q:
                for tt in range(4):
                    e.wait(f"p3.vev{i}.{tt-2}" if tt >= 2 else f"p3.vev{i-1}.{tt+2}",
                           missing_ok=True)
                    for dc in range(8):
                        e.matmul(PS_Z[tt % 2][:, 0:512],
                                 T["hT_sb"][:, i % 2, dc, tt * 128:(tt + 1) * 128],
                                 T["wf2"][:, dc, 0:512], start=(dc == 0), stop=(dc == 7))
                    for dc in range(8):
                        e.matmul(PS_Z[tt % 2][:, 512:1024],
                                 T["hT_sb"][:, i % 2, dc, tt * 128:(tt + 1) * 128],
                                 T["wf2"][:, dc, 512:1024], start=(dc == 0), stop=(dc == 7))
                    e.inc("t_a", f"p3.v{i}.{tt}")

    elif role == "gpsimd":
        for i in range(NP1):
            is_q = i >= NCH
            e.wait(f"p3.kev{i}.7")
            if is_q:
                e.dma_start(out=qT3[:, :, (i - NCH) * CH:(i - NCH + 1) * CH],
                            in_=T["kout"][:, i % 2, :, :])
            else:
                e.dma_start(out=kT3[:, :, i * CH:(i + 1) * CH],
                            in_=T["kout"][:, i % 2, :, :])
            e.inc(f"d_stk{i % 2}", f"p3.kst{i}", 16)
            if not is_q:
                e.wait(f"p3.vev{i}.3")
                e.dma_start(
                    out=dr["v_d"][i * CH:(i + 1) * CH, :].rearrange("(t p) d -> p t d", p=128),
                    in_=T["vout"][:, i % 2, :, :])
                e.inc(f"d_stv{i % 2}", f"p3.vst{i}", 16)
        for sn in ("d_stk0", "d_stk1", "d_stv0", "d_stv1"):
            e.wait_total(sn)



def ph4(ctx, kb, role, e):
    T = ctx["T"]
    dr = ctx["dr"]
    ps_s0, ps_s1, ps_p0, ps_p1, ps_z0, ps_z1 = ctx["psum"]
    if role == "gpsimd":
        for ls in range(16):
            g, j = ls // 2, ls % 2
            src = dr["w1_in"] if j == 0 else dr["w3_in"]
            if ls >= 2:
                e.wait(f"p4.mm{ls-2}")
            e.dma_start(out=T["abuf"][:, ls % 2, :, :],
                        in_=src[g].rearrange("(c p) h -> p c h", p=128))
            e.inc(f"d_abl{ls % 2}", f"p4.w{ls}", 16)
            e.wait(f"p4.aev{ls}")
            tab = T["a_tab"] if j == 0 else T["b_tab"]
            e.dma_start(out=tab[g * 8:(g + 1) * 8, :], in_=T["astage"][0:8, ls % 2, :])
            e.inc(f"d_abs{ls % 2}", f"p4.tst{ls}", 16)
        e.wait("p4.tst14")
        e.wait("p4.tst15")
    elif role == "tensor":
        for ls in range(16):
            e.wait(f"p4.w{ls}")
            e.wait(f"p4.aev{ls-2}", missing_ok=True)
            dst = ps_s0 if ls % 2 == 0 else ps_s1
            for c in range(8):
                e.matmul(dst[0:8, :], T["xf8T_bf"][:, c, :], T["abuf"][:, ls % 2, c, :],
                         start=(c == 0), stop=(c == 7))
            e.inc("t_a", f"p4.mm{ls}")
    elif role == "vector":
        for ls in range(16):
            e.wait(f"p4.mm{ls}")
            e.wait(f"p4.tst{ls-2}", missing_ok=True)
            dst = ps_s0 if ls % 2 == 0 else ps_s1
            e.tensor_copy(T["astage"][0:8, ls % 2, :], dst[0:8, :])
            e.inc("v_a", f"p4.aev{ls}")



def ph5(ctx, kb, role, e):
    T = ctx["T"]
    dr = ctx["dr"]
    ps_s0, ps_s1, ps_p0, ps_p1, ps_z0, ps_z1 = ctx["psum"]
    PS_S = [ps_s0, ps_s1]
    PS_P = [ps_p0, ps_p1]
    PS_Z = [ps_z0, ps_z1]
    kT3 = dr["kT_d"].rearrange("(c p) n -> p c n", p=128)
    qT3 = dr["qT_d"].rearrange("(c p) n -> p c n", p=128)
    att = T["attst"]
    negm = lambda z: att[:, z:z + 1]
    lsum = lambda z: att[:, 8 + z:9 + z]
    alpha = lambda par: att[:, 16 + par:17 + par]
    delta = lambda par: att[:, 18 + par:19 + par]
    lspart = lambda par: att[:, 20 + par:21 + par]
    MX, NN, RL, QREL = 22, 23, 24, 25

    if role == "sync":
        for sw in SWEEPS:
            s = sw["s"]
            if s >= 2:
                e.wait(f"p5.qk{SWEEPS[s-2]['last_pass']}")
            lo_slot = sw["slots"][1]
            e.dma_start(out=T["qt"][:, s % 2, :, :],
                        in_=qT3[:, :, lo_slot * CH:lo_slot * CH + 1024])
            e.inc(f"d_q{s % 2}", f"p5.q{s}", 16)
            for c in [c for c in CHUNKS if c["s"] == s]:
                co = c["ord"]
                if co >= 2:
                    e.wait(f"p5.pv{CHUNK_BY_ORD[co-2]['last_pass']}")
                k = c["k"]
                e.dma_start(out=T["kt"][:, co % 2, :, :],
                            in_=kT3[:, :, k * CH:(k + 1) * CH])
                e.inc(f"d_kv{co % 2}", k=16)
                e.dma_start(
                    out=T["vt"][:, co % 2, :, :],
                    in_=dr["v_d"][k * CH:(k + 1) * CH, :].rearrange("(t p) d -> p t d", p=128))
                e.inc(f"d_kv{co % 2}", f"p5.kv{co}", 16)

    elif role == "tensor":
        def emit_T(p):
            n = p["n"]
            e.wait(f"p5.exp{n}")
            e.wait(f"p5.ptev{n-2}", missing_ok=True)
            for j in range(4):
                e.matmul(PS_P[n % 2][:, j * 128:(j + 1) * 128],
                         T["P_bf"][:, n % 2, j * 128:(j + 1) * 128],
                         T["ident"][:, :], start=True, stop=True)
            e.inc("t_a", f"p5.T{n}")

        def emit_PV(p):
            n = p["n"]
            e.wait(f"p5.ptev{n}")
            e.wait(f"p5.z{n-2}", missing_ok=True)
            co = p["chunk"]
            for j in range(4):
                e.matmul(PS_Z[n % 2][:, 0:512], T["PT_sb"][:, n % 2, j, :],
                         T["vt"][:, co % 2, j, 0:512], start=(j == 0), stop=(j == 3))
            for j in range(4):
                e.matmul(PS_Z[n % 2][:, 512:1024], T["PT_sb"][:, n % 2, j, :],
                         T["vt"][:, co % 2, j, 512:1024], start=(j == 0), stop=(j == 3))
            e.inc("t_a", f"p5.pv{n}")

        for p in PASSES:
            n = p["n"]
            e.wait(f"p5.exp{n-2}", missing_ok=True)
            e.wait(f"p5.kv{p['chunk']}")
            if p["k"] == 0 and n == CHUNK_BY_ORD[p["chunk"]]["first_pass"]:
                e.wait(f"p5.q{p['s']}")
            qoff = (1 - p["qi"]) * CH + p["t"] * 128
            for dc in range(8):
                e.matmul(PS_S[n % 2][:, :], T["qt"][:, p["s"] % 2, dc, qoff:qoff + 128],
                         T["kt"][:, p["chunk"] % 2, dc, :],
                         start=(dc == 0), stop=(dc == 7))
            e.inc("t_a", f"p5.qk{n}")
            if n >= 1:
                emit_T(PASSES[n - 1])
                emit_PV(PASSES[n - 1])
        emit_T(PASSES[-1])
        emit_PV(PASSES[-1])

    elif role == "vector":
        def tail(p):
            n, z = p["n"], p["zslot"]
            e.wait(f"p5.T{n}")
            e.tensor_copy(T["PT_sb"][:, n % 2, :, :], PS_P[n % 2][:, :])
            e.inc("v_a", f"p5.ptev{n}")
            e.wait(f"p5.exp{n}")
            if p["first"]:
                e.tensor_copy(lsum(z), lspart(n % 2))
            else:
                e.scalar_tensor_tensor(lsum(z), lsum(z), alpha(n % 2), lspart(n % 2),
                                       ALU.mult, ALU.add)
            e.wait(f"p5.pv{n}")
            if p["first"]:
                e.tensor_copy(T["Z_sb"][:, z, :], PS_Z[n % 2][:, :])
            else:
                e.scalar_tensor_tensor(T["Z_sb"][:, z, :], T["Z_sb"][:, z, :],
                                       alpha(n % 2), PS_Z[n % 2][:, :],
                                       ALU.mult, ALU.add)
            e.inc("v_a", f"p5.z{n}")
            for fin in FINALS_AT.get(n, []):
                f, fz = fin["f"], fin["zslot"]
                e.fence()
                e.reciprocal(att[:, RL:RL + 1], lsum(fz))
                e.fence()
                e.wait(f"p5.x2st{f-2}", missing_ok=True)
                e.tensor_scalar(T["x2sb"][:, f % 2, :], T["Z_sb"][:, fz, :],
                                att[:, RL:RL + 1], None, ALU.mult)
                e.tensor_tensor(T["x2sb"][:, f % 2, :], T["x2sb"][:, f % 2, :],
                                T["ga_b"][:, :], ALU.mult)
                e.wait(f"p5.xq{f}")
                e.tensor_tensor(T["x2sb"][:, f % 2, :], T["x2sb"][:, f % 2, :],
                                T["xqt"][:, f % 2, :], ALU.add)
                e.inc("v_a", f"p5.fin{f}")

        for p in PASSES:
            n, z = p["n"], p["zslot"]
            e.wait(f"p5.qk{n}")
            if p["masked"]:
                e.tensor_scalar(att[:, QREL:QREL + 1],
                                T["qidx_sb"][:, p["qtile"]:p["qtile"] + 1],
                                float(p["k"] * CH), None, ALU.subtract)
                e.fence()
                e.tensor_scalar(T["mask_sb"][:, :], T["kiota"][:, :],
                                att[:, QREL:QREL + 1], NEG, ALU.is_gt, ALU.mult)
                e.tensor_tensor(PS_S[n % 2][:, :], PS_S[n % 2][:, :],
                                T["mask_sb"][:, :], ALU.add)
                e.fence()
            e.reduce_max(att[:, MX:MX + 1], PS_S[n % 2][:, :], AX.X, negate=True)
            e.fence()
            if p["first"]:
                e.tensor_copy(negm(z), att[:, MX:MX + 1])
            else:
                e.tensor_tensor(att[:, NN:NN + 1], negm(z), att[:, MX:MX + 1], ALU.min)
                e.fence()
                e.tensor_tensor(delta(n % 2), att[:, NN:NN + 1], negm(z), ALU.subtract)
                e.fence()
                e.tensor_copy(negm(z), att[:, NN:NN + 1])
            e.inc("v_a", f"p5.m{n}")
            if n >= 1:
                tail(PASSES[n - 1])
        tail(PASSES[-1])

    elif role == "scalar":
        for p in PASSES:
            n, z = p["n"], p["zslot"]
            e.wait(f"p5.m{n}")
            if not p["first"]:
                e.activation(alpha(n % 2), delta(n % 2), AF.Exp)
            e.wait(f"p5.T{n-2}", missing_ok=True)
            e.activation(T["P_bf"][:, n % 2, :], PS_S[n % 2][:, :], AF.Exp,
                         bias=negm(z), accum_out=lspart(n % 2))
            e.inc("s_a", f"p5.exp{n}")

    elif role == "gpsimd":
        def load(f):
            fin = FINALS[f]
            e.wait(f"p5.fin{f-2}", missing_ok=True)
            e.dma_start(out=T["xqt"][:, f % 2, :],
                        in_=dr["x_q"][fin["qtile"] * 128:(fin["qtile"] + 1) * 128, :])
            e.inc(f"d_xq{f % 2}", f"p5.xq{f}", 16)

        load(0)
        load(1)
        for f in range(32):
            e.wait(f"p5.fin{f}")
            fin = FINALS[f]
            e.dma_start(out=dr["x2_d"][fin["qtile"] * 128:(fin["qtile"] + 1) * 128, :],
                        in_=T["x2sb"][:, f % 2, :])
            e.inc(f"d_x2{f % 2}", f"p5.x2st{f}", 16)
            if f + 2 < 32:
                load(f + 2)
        e.wait_total("d_x20")
        e.wait_total("d_x21")



def ph6(ctx, kb, role, e):
    T = ctx["T"]
    dr = ctx["dr"]
    ps_s0, ps_s1, ps_p0, ps_p1, ps_z0, ps_z1 = ctx["psum"]
    WSRC = ["sw1T", "sw3T", "sw2T"]
    WDST = ["wf0", "wf1", "wf2"]

    if role == "sync":
        for w in range(3):
            for c in range(8):
                ci = w * 8 + c
                if ci >= 2:
                    e.wait(f"p6.fold{ci-2}")
                e.dma_start(out=T["wstage"][:, ci % 2, :],
                            in_=dr[WSRC[w]].rearrange("(c p) n -> p c n", p=128)[:, c, :])
                e.inc(f"d_ws{ci % 2}", f"p6.ws{ci}", 16)

    elif role == "gpsimd":
        e.dma_start(out=T["wg_raw"][:, :, :],
                    in_=dr["wgT"].rearrange("(c p) n -> p c n", p=128))
        e.inc("d_g0", "p6.wg", 16)

    elif role == "vector":
        for w in range(3):
            for c in range(8):
                ci = w * 8 + c
                e.wait(f"p6.ws{ci}")
                if w < 2:
                    e.tensor_scalar(T[WDST[w]][:, c, :], T["wstage"][:, ci % 2, :],
                                    T["cb1_col"][:, c:c + 1], None, ALU.mult)
                    e.wait(f"p6.mm{ci-2}", missing_ok=True)
                    e.tensor_copy(T["raw_bf"][:, ci % 2, :], T["wstage"][:, ci % 2, :])
                else:
                    e.tensor_copy(T[WDST[w]][:, c, :], T["wstage"][:, ci % 2, :])
                e.inc("v_a", f"p6.fold{ci}")
            if w < 2:
                e.wait(f"p6.mm{w*8+7}")
                dst = T["t1sb"][0:1, :] if w == 0 else T["outsb"][0:1, 0, :]
                e.tensor_copy(dst, ps_z0[0:1, :])
                e.inc("v_a", f"p6.brow{w}")
        e.wait("p6.wg")
        for c in range(8):
            e.tensor_scalar(T["wg_f"][:, c, :], T["wg_raw"][:, c, :],
                            T["cb1_col"][:, c:c + 1], None, ALU.mult)
        e.inc("v_a", "p6.wgf")
        e.wait("p6.bgmm")
        e.tensor_copy(T["outsb"][0:1, 1, 0:8], ps_z1[0:1, 0:8])
        e.inc("v_a", "p6.bgrow")

    elif role == "tensor":
        for w in range(2):
            for c in range(8):
                ci = w * 8 + c
                e.wait(f"p6.fold{ci}")
                if c == 0 and w > 0:
                    e.wait(f"p6.brow{w-1}")
                e.matmul(ps_z0[0:1, 0:512], T["sb_col_bf"][:, c:c + 1],
                         T["raw_bf"][:, ci % 2, 0:512], start=(c == 0), stop=(c == 7))
                e.matmul(ps_z0[0:1, 512:1024], T["sb_col_bf"][:, c:c + 1],
                         T["raw_bf"][:, ci % 2, 512:1024], start=(c == 0), stop=(c == 7))
                e.inc("t_a", f"p6.mm{ci}")
        e.wait("p6.wg")
        e.wait("p6.brow1")
        for c in range(8):
            e.matmul(ps_z1[0:1, 0:8], T["sb_col_bf"][:, c:c + 1], T["wg_raw"][:, c, :],
                     start=(c == 0), stop=(c == 7))
        e.inc("t_a", "p6.bgmm")


def ph6b(ctx, kb, role, e):
    T = ctx["T"]
    ps_s0, ps_s1, ps_p0, ps_p1, ps_z0, ps_z1 = ctx["psum"]
    if role == "tensor":
        e.matmul(ps_s0[:, :], T["onesrow"][0:1, :], T["t1sb"][0:1, 0:512],
                 start=True, stop=True)
        e.matmul(ps_s1[:, :], T["onesrow"][0:1, :], T["t1sb"][0:1, 512:1024],
                 start=True, stop=True)
        e.inc("t_a", "p6.bb1")
        e.wait("p6.eb1")
        e.matmul(ps_s0[:, :], T["onesrow"][0:1, :], T["outsb"][0:1, 0, 0:512],
                 start=True, stop=True)
        e.matmul(ps_s1[:, :], T["onesrow"][0:1, :], T["outsb"][0:1, 0, 512:1024],
                 start=True, stop=True)
        e.inc("t_a", "p6.bb3")
        e.wait("p6.eb3")
        e.matmul(ps_p0[:, 0:8], T["onesrow"][0:1, :], T["outsb"][0:1, 1, 0:8],
                 start=True, stop=True)
        e.inc("t_a", "p6.bbg")
    elif role == "scalar":
        e.wait("p6.bb1")
        e.copy(T["b1_b"][:, 0:512], ps_s0[:, :])
        e.copy(T["b1_b"][:, 512:1024], ps_s1[:, :])
        e.inc("s_a", "p6.eb1")
        e.wait("p6.bb3")
        e.copy(T["b3_b"][:, 0:512], ps_s0[:, :])
        e.copy(T["b3_b"][:, 512:1024], ps_s1[:, :])
        e.inc("s_a", "p6.eb3")
        e.wait("p6.bbg")
        e.copy(T["bg_b"][:, :], ps_p0[:, 0:8])
        e.inc("s_a", "p6.ebg")



def ph7(ctx, kb, role, e):
    T = ctx["T"]
    dr = ctx["dr"]
    ps_s0, ps_s1, ps_p0, ps_p1, ps_z0, ps_z1 = ctx["psum"]
    PS_P = [ps_p0, ps_p1]

    if role == "sync":
        for tl in range(NTILE):
            e.wait(f"p7.out{tl-2}", missing_ok=True)
            e.dma_start(out=T["x2f"][:, tl % 2, :],
                        in_=dr["x2_d"][tl * 128:(tl + 1) * 128, :])
            e.inc(f"d_x2l{tl % 2}", f"p7.x2{tl}", 16)

    elif role == "scalar":
        for tl in range(NTILE):
            stt = lambda j: T["fstats"][:, tl % 2, j:j + 1]
            e.wait(f"p7.x2{tl}")
            e.activation(T["fsq"][:, :], T["x2f"][:, tl % 2, :], AF.Square,
                         accum_out=stt(0))
            e.inc("s_a", f"p7.sq{tl}")
            e.wait(f"p7.va{tl}")
            e.activation(stt(5), stt(2), AF.Sqrt)
            e.inc("s_a", f"p7.sqrt{tl}")
            e.wait(f"p7.vr{tl}")
            e.wait(f"p7.tr{tl-2}.1", missing_ok=True)
            e.activation(T["lnb"][:, tl % 2, :], T["x2f"][:, tl % 2, :], AF.Identity,
                         bias=stt(4), scale=stt(3))
            e.inc("s_a", f"p7.ln{tl}")
            for r in range(2):
                e.wait(f"p7.tr{tl}.{r}")
                e.copy(T["lnT"][:, tl % 2, 4 * r:4 * r + 4, :], PS_P[r][:, :])
                e.inc("s_a", f"p7.lnTev{tl}.{r}")
            e.wait(f"p7.t1{tl}")
            e.activation(T["sgl"][:, :], T["t1sb"][:, :], AF.Silu)
            e.inc("s_a", f"p7.silu{tl}")
            for r in range(2):
                e.wait(f"p7.ptr{tl}.{r}")
                e.copy(T["prodT"][:, tl % 2, 4 * r:4 * r + 4, :], PS_P[r][:, :])
                e.inc("s_a", f"p7.pev{tl}.{r}")

    elif role == "vector":
        for tl in range(NTILE):
            stt = lambda j: T["fstats"][:, tl % 2, j:j + 1]
            e.wait(f"p7.x2{tl}")
            e.reduce_sum(stt(1), T["x2f"][:, tl % 2, :], AX.X)
            e.wait(f"p7.sq{tl}")
            ln_stats_vector(e, stt)
            e.inc("v_a", f"p7.va{tl}")
            e.wait(f"p7.sqrt{tl}")
            ln_polish_vector(e, stt)
            e.inc("v_a", f"p7.vr{tl}")
            e.wait(f"p7.g1{tl}")
            e.tensor_tensor(T["t1sb"][:, :], ps_z0[:, :], T["b1_b"][:, :], ALU.add)
            e.inc("v_a", f"p7.t1{tl}")
            e.wait(f"p7.g3{tl}")
            e.tensor_tensor(T["t3bf"][:, :], ps_z1[:, :], T["b3_b"][:, :], ALU.add)
            e.inc("v_a", f"p7.t3{tl}")
            e.wait(f"p7.lg{tl}")
            e.tensor_tensor(T["logits_sb"][:, tl, :], ps_p1[:, 0:8], T["bg_b"][:, :],
                            ALU.add)
            e.inc("v_a", f"p7.lgev{tl}")
            e.wait(f"p7.silu{tl}")
            e.tensor_tensor(T["prod"][:, :], T["sgl"][:, :], T["t3bf"][:, :], ALU.mult)
            e.inc("v_a", f"p7.prod{tl}")
            e.wait(f"p7.sw2{tl}")
            e.wait(f"p7.yst{tl-2}", missing_ok=True)
            if tl == 0:
                e.tensor_copy(T["sh8_keep"][0:8, 0:512], ps_s0[0:8, :])
                e.tensor_copy(T["sh8_keep"][0:8, 512:1024], ps_s1[0:8, :])
                e.tensor_copy(T["x28_keep"][0:8, :], T["x2f"][0:8, 0, :])
            e.scalar_tensor_tensor(T["outsb"][:, tl % 2, 0:512], ps_s0[:, :], 0.0,
                                   T["gb_b"][:, 0:512], ALU.add, ALU.mult)
            e.tensor_tensor(T["outsb"][:, tl % 2, 0:512], T["outsb"][:, tl % 2, 0:512],
                            T["x2f"][:, tl % 2, 0:512], ALU.add)
            e.scalar_tensor_tensor(T["outsb"][:, tl % 2, 512:1024], ps_s1[:, :], 0.0,
                                   T["gb_b"][:, 512:1024], ALU.add, ALU.mult)
            e.tensor_tensor(T["outsb"][:, tl % 2, 512:1024],
                            T["outsb"][:, tl % 2, 512:1024],
                            T["x2f"][:, tl % 2, 512:1024], ALU.add)
            e.inc("v_a", f"p7.out{tl}")

    elif role == "tensor":
        for tl in range(NTILE):
            for r in range(2):
                e.wait(f"p7.ln{tl}")
                e.wait(f"p7.pev{tl-1}.{r}", missing_ok=True)
                if r == 1:
                    e.wait(f"p7.lgev{tl-1}", missing_ok=True)
                for cc in range(4):
                    c = 4 * r + cc
                    e.matmul(PS_P[r][:, cc * 128:(cc + 1) * 128],
                             T["lnb"][:, tl % 2, c * 128:(c + 1) * 128],
                             T["ident"][:, :], start=True, stop=True)
                e.inc("t_a", f"p7.tr{tl}.{r}")
            e.wait(f"p7.lnTev{tl}.1")
            e.wait(f"p7.t1{tl-1}", missing_ok=True)
            for dc in range(8):
                e.matmul(ps_z0[:, 0:512], T["lnT"][:, tl % 2, dc, :],
                         T["wf0"][:, dc, 0:512], start=(dc == 0), stop=(dc == 7))
            for dc in range(8):
                e.matmul(ps_z0[:, 512:1024], T["lnT"][:, tl % 2, dc, :],
                         T["wf0"][:, dc, 512:1024], start=(dc == 0), stop=(dc == 7))
            e.inc("t_a", f"p7.g1{tl}")
            e.wait(f"p7.t3{tl-1}", missing_ok=True)
            for dc in range(8):
                e.matmul(ps_z1[:, 0:512], T["lnT"][:, tl % 2, dc, :],
                         T["wf1"][:, dc, 0:512], start=(dc == 0), stop=(dc == 7))
            for dc in range(8):
                e.matmul(ps_z1[:, 512:1024], T["lnT"][:, tl % 2, dc, :],
                         T["wf1"][:, dc, 512:1024], start=(dc == 0), stop=(dc == 7))
            e.inc("t_a", f"p7.g3{tl}")
            e.wait(f"p7.lnTev{tl}.1")
            for dc in range(8):
                e.matmul(ps_p1[:, 0:8], T["lnT"][:, tl % 2, dc, :],
                         T["wg_f"][:, dc, :], start=(dc == 0), stop=(dc == 7))
            e.inc("t_a", f"p7.lg{tl}")
            for r in range(2):
                e.wait(f"p7.prod{tl}")
                e.wait(f"p7.lnTev{tl}.{r}")
                if r == 1:
                    e.wait(f"p7.lgev{tl}")
                for cc in range(4):
                    c = 4 * r + cc
                    e.matmul(PS_P[r][:, cc * 128:(cc + 1) * 128],
                             T["prod"][:, c * 128:(c + 1) * 128],
                             T["ident"][:, :], start=True, stop=True)
                e.inc("t_a", f"p7.ptr{tl}.{r}")
            e.wait(f"p7.pev{tl}.1")
            e.wait(f"p7.out{tl-1}", missing_ok=True)
            for dc in range(8):
                e.matmul(ps_s0[:, :], T["prodT"][:, tl % 2, dc, :],
                         T["wf2"][:, dc, 0:512], start=(dc == 0), stop=(dc == 7))
            for dc in range(8):
                e.matmul(ps_s1[:, :], T["prodT"][:, tl % 2, dc, :],
                         T["wf2"][:, dc, 512:1024], start=(dc == 0), stop=(dc == 7))
            e.inc("t_a", f"p7.sw2{tl}")

    elif role == "gpsimd":
        for tl in range(NTILE):
            e.wait(f"p7.out{tl}")
            e.dma_start(out=dr["y"][tl * 128:(tl + 1) * 128, :],
                        in_=T["outsb"][:, tl % 2, :])
            e.inc(f"d_yst{tl % 2}", f"p7.yst{tl}", 16)



def ph8(ctx, kb, role, e):
    T = ctx["T"]
    dr = ctx["dr"]
    ps_s0, ps_s1, ps_p0, ps_p1, ps_z0, ps_z1 = ctx["psum"]
    L = T["logits_sb"]

    if role == "vector":
        e.reduce_max(T["mx_r"][:, :, None], L[:, :, :], AX.X)
        e.fence()
        e.tensor_tensor(T["scor"][:, :, :], L[:, :, :],
                        T["mx_r"][:, :, None].broadcast_to([128, NTILE, 8]),
                        ALU.subtract)
        e.inc("v_a", "p8.sub")
        e.wait("p8.exp")
        e.reduce_sum(T["mx_r"][:, :, None], T["scor"][:, :, :], AX.X)
        e.fence()
        e.reciprocal(T["mx_r"][:, :], T["mx_r"][:, :])
        e.fence()
        e.tensor_tensor(T["scor"][:, :, :], T["scor"][:, :, :],
                        T["mx_r"][:, :, None].broadcast_to([128, NTILE, 8]), ALU.mult)
        e.tensor_tensor(T["sbias"][:, :, :], T["scor"][:, :, :],
                        T["eb_b"][:, None, :].broadcast_to([128, NTILE, 8]), ALU.add)
        e.fence()
        sp = T["sbias"].rearrange("p t (g two) -> p t g two", two=2)
        e.tensor_reduce(T["grp"][:, :, :, None], sp, AX.X, ALU.add)
        e.fence()
        A4 = T["grp"][:, :, :, None].broadcast_to([128, NTILE, 4, 4])
        B4 = T["grp"][:, :, None, :].broadcast_to([128, NTILE, 4, 4])
        C4 = T["cmp"][:, :, 0:4, 0:4]
        D4 = T["cmp2"][:, :, 0:4, 0:4]
        e.tensor_tensor(C4, B4, A4, ALU.is_gt)
        e.tensor_tensor(D4, B4, A4, ALU.is_equal)
        e.tensor_tensor(D4, D4, T["tie4"][:, None, :, :].broadcast_to([128, NTILE, 4, 4]),
                        ALU.mult)
        e.tensor_tensor(C4, C4, D4, ALU.add)
        e.tensor_reduce(T["rank4"][:, :, :, None], C4, AX.X, ALU.add)
        e.fence()
        e.tensor_single_scalar(T["keep4"][:, :, :], T["rank4"][:, :, :], 2.0, ALU.is_lt)
        e.fence()
        k8v = T["keep8"].rearrange("p t (g two) -> p t g two", two=2)
        e.tensor_copy(k8v, T["keep4"][:, :, :, None].broadcast_to([128, NTILE, 4, 2]))
        e.fence()
        e.tensor_tensor(T["maskt"][:, :, :], T["sbias"][:, :, :], T["keep8"][:, :, :],
                        ALU.mult)
        e.fence()
        e.tensor_scalar(T["keep8"][:, :, :], T["keep8"][:, :, :], 1.0e30, 1.0e30,
                        ALU.mult, ALU.subtract)
        e.fence()
        e.tensor_tensor(T["maskt"][:, :, :], T["maskt"][:, :, :], T["keep8"][:, :, :],
                        ALU.add)
        e.fence()
        A8 = T["maskt"][:, :, :, None].broadcast_to([128, NTILE, 8, 8])
        B8 = T["maskt"][:, :, None, :].broadcast_to([128, NTILE, 8, 8])
        e.tensor_tensor(T["cmp"][:, :, :, :], B8, A8, ALU.is_gt)
        e.tensor_tensor(T["cmp2"][:, :, :, :], B8, A8, ALU.is_equal)
        e.tensor_tensor(T["cmp2"][:, :, :, :], T["cmp2"][:, :, :, :],
                        T["tie8"][:, None, :, :].broadcast_to([128, NTILE, 8, 8]),
                        ALU.mult)
        e.tensor_tensor(T["cmp"][:, :, :, :], T["cmp"][:, :, :, :],
                        T["cmp2"][:, :, :, :], ALU.add)
        e.tensor_reduce(T["rank8"][:, :, :, None], T["cmp"][:, :, :, :], AX.X, ALU.add)
        e.fence()
        e.tensor_single_scalar(T["oh1"][:, :, :], T["rank8"][:, :, :], 0.0, ALU.is_equal)
        e.tensor_single_scalar(T["oh2"][:, :, :], T["rank8"][:, :, :], 1.0, ALU.is_equal)
        e.fence()
        IO8 = T["iota8"][:, None, :].broadcast_to([128, NTILE, 8])
        e.tensor_tensor(T["maskt"][:, :, :], T["oh1"][:, :, :], IO8, ALU.mult)
        e.fence()
        e.tensor_reduce(T["e1_sb"][:, :, None], T["maskt"][:, :, :], AX.X, ALU.add)
        e.fence()
        e.tensor_tensor(T["maskt"][:, :, :], T["oh2"][:, :, :], IO8, ALU.mult)
        e.fence()
        e.tensor_reduce(T["e2_sb"][:, :, None], T["maskt"][:, :, :], AX.X, ALU.add)
        e.fence()
        e.tensor_tensor(T["maskt"][:, :, :], T["oh1"][:, :, :], T["scor"][:, :, :],
                        ALU.mult)
        e.fence()
        e.tensor_reduce(T["wt1_sb"][:, :, None], T["maskt"][:, :, :], AX.X, ALU.add)
        e.fence()
        e.tensor_tensor(T["maskt"][:, :, :], T["oh2"][:, :, :], T["scor"][:, :, :],
                        ALU.mult)
        e.fence()
        e.tensor_reduce(T["wt2_sb"][:, :, None], T["maskt"][:, :, :], AX.X, ALU.add)
        e.fence()
        e.tensor_tensor(T["ohsum"].rearrange("p (t e) -> p t e", e=8),
                        T["oh1"][:, :, :], T["oh2"][:, :, :], ALU.add)
        e.inc("v_a", "p8.oh")
        e.wait("p8.cntmm")
        e.tensor_copy(T["cnt_row"][0:1, :], ps_p0[0:1, 0:256])
        e.fence()
        cr = T["cnt_row"].rearrange("p (t e) -> p e t", e=8)
        e.tensor_reduce(T["cnt8"][0:1, :, None], cr, AX.X, ALU.add)
        e.fence()
        e.inc("v_a", "p8.cnt")
        e.wait("p8.cntcol")
        e.tensor_copy(T["cnt_col"][0:8, :], ps_s0[0:8, 0:1])
        e.inc("v_a", "p8.ecntcol")
        e.wait("p8.offsmm")
        e.tensor_copy(T["offs_col"][0:8, :], ps_s1[0:8, 0:1])
        e.inc("v_a", "p8.eoffscol")
        e.wait("p8.offsrow")
        e.tensor_copy(T["offs_row"][0:1, :], ps_p0[0:1, 0:8])
        e.inc("v_a", "p8.eoffsrow")
        e.wait("p8.offsb")
        e.tensor_copy(T["offs_b"][:, :], ps_p1[:, 0:8])
        e.inc("v_a", "p8.eoffsb")

    elif role == "scalar":
        e.wait("p8.sub")
        e.activation(T["scor"][:, :, :], T["scor"][:, :, :], AF.Exp)
        e.inc("s_a", "p8.exp")

    elif role == "tensor":
        e.wait("p8.oh")
        e.matmul(ps_p0[0:1, 0:256], T["onescol"][:, 0:1], T["ohsum"][:, 0:256],
                 start=True, stop=True)
        e.inc("t_a", "p8.cntmm")
        e.wait("p8.ccret")
        e.matmul(ps_s0[0:8, 0:1], T["cnt8"][0:1, 0:8], T["ones11"][0:1, 0:1],
                 start=True, stop=True)
        e.inc("t_a", "p8.cntcol")
        e.wait("p8.ecntcol")
        e.matmul(ps_s1[0:8, 0:1], T["tri8"][0:8, 0:8], T["cnt_col"][0:8, 0:1],
                 start=True, stop=True)
        e.inc("t_a", "p8.offsmm")
        e.wait("p8.eoffscol")
        e.matmul(ps_p0[0:1, 0:8], T["offs_col"][0:8, 0:1], T["i8s"][0:8, 0:8],
                 start=True, stop=True)
        e.inc("t_a", "p8.offsrow")
        e.wait("p8.eoffsrow")
        e.matmul(ps_p1[:, 0:8], T["onesrow"][0:1, :], T["offs_row"][0:1, 0:8],
                 start=True, stop=True)
        e.inc("t_a", "p8.offsb")

    elif role == "gpsimd":
        e.wait("p8.cnt")
        e.dma_start(out=dr["cc_cnt_i"][0:1, :], in_=T["cnt8"][0:1, :])
        e.inc("d_cc", "p8.ccin", 16)
        e.wait("p8.ccin")
        e.collective_compute("AllReduce", ALU.add, replica_groups=RG,
                             ins=[dr["cc_cnt_i"][:, :]], outs=[dr["cc_cnt_o"][:, :]])
        e.inc("cc_done", "p8.cc")
        e.wait("p8.cc")
        e.dma_start(out=T["cnt8"][0:1, :], in_=dr["cc_cnt_o"][0:1, :])
        e.inc("d_cc", "p8.ccret", 16)



def _term(ctx, e, jt):
    T = ctx["T"]
    ps_s0, ps_s1, ps_p0, ps_p1, ps_z0, ps_z1 = ctx["psum"]
    PS_B = [ps_p0, ps_p1]
    par = jt % 2
    e.wait(f"p9.silu{jt}")
    e.wait(f"p9.B{jt}")
    e.scalar_tensor_tensor(T["term"][:, par, :], PS_B[par][:, :],
                           T["w64"][:, jt:jt + 1],
                           T["siluh"][:, par, :], ALU.mult, ALU.mult)
    e.inc("v_a", f"p9.term{jt}")


def ph9(ctx, kb, role, e):
    T = ctx["T"]
    dr = ctx["dr"]
    ps_s0, ps_s1, ps_p0, ps_p1, ps_z0, ps_z1 = ctx["psum"]
    PS_A = [ps_s0, ps_s1]

    if role == "vector":
        jj2 = T["jj64"].rearrange("p (t two) -> p t two", two=2)
        e.tensor_scalar(jj2[:, :, 0], T["nidx_sb"][:, :], 2.0, None, ALU.mult)
        e.tensor_scalar(jj2[:, :, 1], T["nidx_sb"][:, :], 2.0, 1.0,
                        ALU.mult, ALU.add)
        e.fence()
        e.tensor_tensor(T["cmpj"][:, :, :],
                        T["offs_b"][:, None, :].broadcast_to([128, 64, 8]),
                        T["jj64"][:, :, None].broadcast_to([128, 64, 8]),
                        ALU.is_le)
        e.fence()
        e.tensor_reduce(T["jj64"][:, :, None], T["cmpj"][:, :, :], AX.X, ALU.add)
        e.fence()
        ee2 = T["e64"].rearrange("p (t two) -> p t two", two=2)
        e.tensor_copy(ee2[:, :, 0], T["e1_sb"][:, :])
        e.tensor_copy(ee2[:, :, 1], T["e2_sb"][:, :])
        ww2 = T["w64"].rearrange("p (t two) -> p t two", two=2)
        e.tensor_copy(ww2[:, :, 0], T["wt1_sb"][:, :])
        e.tensor_copy(ww2[:, :, 1], T["wt2_sb"][:, :])
        e.fence()
        e.scalar_tensor_tensor(T["row10"][:, :], T["jj64"][:, :], 8.0,
                               T["e64"][:, :], ALU.mult, ALU.add)
        e.fence()
        for jt in range(64):
            par = jt % 2
            e.wait(f"p9.H{jt-2}", missing_ok=True)
            e.tensor_single_scalar(T["ohT"][:, par, :], T["iota64"][:, :],
                                   T["row10"][:, jt:jt + 1], ALU.is_equal)
            e.inc("v_a", f"p9.oht{jt}")
            if jt >= 1:
                _term(ctx, e, jt - 1)
        _term(ctx, e, 63)
        e.wait("p9.H63")
        e.tensor_copy(T["h_f"][0:64, :], ps_z0[0:64, 0:512])
        e.inc("v_a", "p9.hev")

    elif role == "scalar":
        for jt in range(64):
            par = jt % 2
            e.wait(f"p9.tro{jt}")
            e.copy(T["ohs"][0:64, par, :], ps_z1[0:64, par * 512:par * 512 + 128])
            e.inc("s_a", f"p9.ohev{jt}")
            e.wait(f"p9.A{jt}")
            e.activation(T["siluh"][:, par, :], PS_A[par][:, :], AF.Silu,
                         scale=T["w64"][:, jt:jt + 1])
            e.inc("s_a", f"p9.silu{jt}")

    elif role == "tensor":
        PS_B = [ps_p0, ps_p1]
        for jt in range(64):
            par = jt % 2
            e.wait(f"p9.oht{jt}")
            e.wait(f"p9.ohev{jt-2}", missing_ok=True)
            e.matmul(ps_z1[0:64, par * 512:par * 512 + 128], T["ohT"][:, par, :],
                     T["ident"][:, :], start=True, stop=True)
            e.inc("t_a", f"p9.tro{jt}")
            e.wait(f"p9.ohev{jt}")
            e.wait(f"p9.silu{jt-2}", missing_ok=True)
            e.matmul(PS_A[par][:, :], T["ohs"][0:64, par, :], T["a_tab"][0:64, :],
                     start=True, stop=True)
            e.inc("t_a", f"p9.A{jt}")
            e.wait(f"p9.term{jt-2}", missing_ok=True)
            e.matmul(PS_B[par][:, :], T["ohs"][0:64, par, :], T["b_tab"][0:64, :],
                     start=True, stop=True)
            e.inc("t_a", f"p9.B{jt}")
            e.wait(f"p9.term{jt}")
            e.matmul(ps_z0[0:64, 0:512], T["ohT"][:, par, :], T["term"][:, par, :],
                     start=(jt == 0), stop=(jt == 63))
            e.inc("t_a", f"p9.H{jt}")

    elif role == "gpsimd":
        e.dma_start(out=T["w2buf"][:, 0, :, :],
                    in_=dr["w2_in"][0].rearrange("(c p) d -> p c d", p=128))
        e.inc("d_w20", "p10.w2_0", 16)
        e.dma_start(out=T["w2buf"][:, 1, :, :],
                    in_=dr["w2_in"][1].rearrange("(c p) d -> p c d", p=128))
        e.inc("d_w21", "p10.w2_1", 16)
        e.wait("p9.hev")
        e.dma_start(out=dr["cc_h_i"][:, :], in_=T["h_f"][0:64, :])
        e.inc("d_cc", "p9.ccin", 16)
        e.wait("p9.ccin")
        e.collective_compute("AllReduce", ALU.add, replica_groups=RG,
                             ins=[dr["cc_h_i"][:, :]], outs=[dr["cc_h_o"][:, :]])
        e.inc("cc_done", "p9.cc")
        e.wait("p9.cc")
        e.dma_start(out=T["h_f"][0:64, :], in_=dr["cc_h_o"][:, :])
        e.inc("d_cc", "p9.hret", 16)



def ph10(ctx, kb, role, e):
    T = ctx["T"]
    dr = ctx["dr"]
    ps_s0, ps_s1, ps_p0, ps_p1, ps_z0, ps_z1 = ctx["psum"]

    if role == "vector":
        e.wait("p9.hret")
        e.tensor_copy(T["h_bf"][0:64, :], T["h_f"][0:64, :])
        e.inc("v_a", "p10.hbf")
        e.wait("p10.r8g7")
        e.tensor_copy(T["r8"][0:8, :], ps_z0[0:8, :])
        e.scalar_tensor_tensor(T["out8"][0:8, :], T["r8"][0:8, :], T["own8"][0:8, 0:1],
                               T["sh8_keep"][0:8, :], ALU.mult, ALU.add)
        e.tensor_tensor(T["out8"][0:8, :], T["out8"][0:8, :], T["gb_b"][0:8, :],
                        ALU.mult)
        e.tensor_tensor(T["out8"][0:8, :], T["out8"][0:8, :], T["x28_keep"][0:8, :],
                        ALU.add)
        e.inc("v_a", "p10.out8")

    elif role == "tensor":
        e.wait("p10.hbf")
        for c in range(4):
            e.matmul(ps_p0[:, c * 64:(c + 1) * 64],
                     T["h_bf"][0:64, c * 128:(c + 1) * 128],
                     T["ident"][0:64, 0:64], start=True, stop=True)
        e.inc("t_a", "p10.ht")
        e.wait("p10.htev")
        first = True
        for g in range(8):
            e.wait(f"p10.w2_{g}")
            for hc in range(4):
                lhs = T["HT_sb"][:, hc, g * 8:(g + 1) * 8]
                e.matmul(ps_z0[0:8, 0:512], lhs, T["w2buf"][:, g % 2, hc, 0:512],
                         start=first, stop=(g == 7 and hc == 3))
                e.matmul(ps_z0[0:8, 512:1024], lhs, T["w2buf"][:, g % 2, hc, 512:1024],
                         start=first, stop=(g == 7 and hc == 3))
                first = False
            e.inc("t_a", f"p10.r8g{g}")

    elif role == "scalar":
        e.wait("p10.ht")
        e.copy(T["HT_sb"][:, :, :], ps_p0[:, 0:256])
        e.inc("s_a", "p10.htev")

    elif role == "gpsimd":
        for g in range(2, 8):
            e.wait(f"p10.r8g{g-2}")
            e.dma_start(out=T["w2buf"][:, g % 2, :, :],
                        in_=dr["w2_in"][g].rearrange("(c p) d -> p c d", p=128))
            e.inc(f"d_w2{g % 2}", f"p10.w2_{g}", 16)
        e.wait("p10.out8")
        e.dma_start(out=dr["y"][0:8, :], in_=T["out8"][0:8, :])
        e.inc("d_yfx", "p10.yfix", 16)
        for sn in ("d_yst0", "d_yst1", "d_yfx"):
            e.wait_total(sn)


PHASES = [ph0, ph1, ph2, ph3, ph4, ph5, ph6, ph6b, ph7, ph8, ph9, ph10]
DEBUG = None


def make_debug_phase(dumps):
    def phdbg(ctx, kb, role, e):
        if role != "gpsimd":
            return
        off = 0
        for (nm, rows, cols) in dumps:
            src = ctx["T"][nm]
            if len(src.shape) == 3:
                flat = src.rearrange("p a b -> p (a b)")
            elif len(src.shape) == 4:
                flat = src.rearrange("p a b c -> p (a b c)")
            else:
                flat = src
            e.dma_start(out=ctx["dr"]["dbg"][0:rows, off:off + cols],
                        in_=flat[0:rows, 0:cols])
            e.inc("d_dbg", k=16)
            off += cols
        e.wait_total("d_dbg")
    return phdbg


def build_nc():
    nc = bass.Bass("TRN2", target_bir_lowering=False, detect_race_conditions=False)

    dr = {}
    for name, shape in [
        ("x_q", [NQ, D]), ("x_kv", [S, D]), ("x8", [8, D]),
        ("qidx", [128, NTILE]), ("nidx", [128, NTILE]), ("cT", [D, 2]),
        ("wmod", [D, 6 * D]), ("wqT", [D, D]), ("wkT", [D, D]), ("wvT", [D, D]),
        ("bq_r", [1, D]), ("bk_r", [1, D]), ("bv_r", [1, D]),
        ("bq8", [8, D]), ("bk8", [8, D]), ("bv8", [8, D]),
        ("sw1T", [D, D]), ("sw2T", [D, D]), ("sw3T", [D, D]), ("wgT", [D, 8]),
        ("eb_in", [128, 8]), ("w1_in", [8, D, 512]), ("w3_in", [8, D, 512]),
        ("w2_in", [8, 512, D]), ("sel_in", [128, 2]), ("own_in", [8, 1]),
    ]:
        dr[name] = nc.dram_tensor(name, shape, F32, kind="ExternalInput").ap()
    dr["y"] = nc.dram_tensor("y", [NQ, D], F32, kind="ExternalOutput").ap()
    if DEBUG is not None:
        dr["dbg"] = nc.dram_tensor("dbg", [128, 65536], F32, kind="ExternalOutput").ap()
    dr["kT_d"] = nc.dram_tensor("kT_d", [D, S], BF16, kind="Internal").ap()
    dr["v_d"] = nc.dram_tensor("v_d", [S, D], BF16, kind="Internal").ap()
    dr["qT_d"] = nc.dram_tensor("qT_d", [D, NQ], BF16, kind="Internal").ap()
    dr["x2_d"] = nc.dram_tensor("x2_d", [NQ, D], F32, kind="Internal").ap()
    dr["cc_cnt_i"] = nc.dram_tensor("cc_cnt_i", [1, 8], F32, kind="Internal").ap()
    dr["cc_cnt_o"] = nc.dram_tensor("cc_cnt_o", [1, 8], F32, kind="Internal").ap()
    dr["cc_h_i"] = nc.dram_tensor("cc_h_i", [64, 512], F32, kind="Internal").ap()
    dr["cc_h_o"] = nc.dram_tensor("cc_h_o", [64, 512], F32, kind="Internal").ap()

    tie8_np = np.fromfunction(lambda j, i: (i < j).astype(np.float32), (8, 8))
    dr["ident_c"] = nc.inline_tensor(np.eye(128, dtype=BF), name="ident_c").ap()
    dr["ones11_c"] = nc.inline_tensor(np.ones((1, 1), np.float32), name="ones11_c").ap()
    dr["onesrow_c"] = nc.inline_tensor(np.ones((1, 128), np.float32), name="onesrow_c").ap()
    dr["onescol_c"] = nc.inline_tensor(np.ones((128, 1), dtype=BF), name="onescol_c").ap()
    dr["kiota_c"] = nc.inline_tensor(
        np.broadcast_to(np.arange(512, dtype=np.float32)[None, :], (128, 512)).copy(),
        name="kiota_c").ap()
    dr["iota8_c"] = nc.inline_tensor(
        np.broadcast_to(np.arange(8, dtype=np.float32)[None, :], (128, 8)).copy(),
        name="iota8_c").ap()
    dr["iota64_c"] = nc.inline_tensor(
        np.broadcast_to(np.arange(64, dtype=np.float32)[None, :], (128, 64)).copy(),
        name="iota64_c").ap()
    dr["tie8_c"] = nc.inline_tensor(
        np.broadcast_to(tie8_np[None], (128, 8, 8)).copy(), name="tie8_c").ap()
    dr["tie4_c"] = nc.inline_tensor(
        np.broadcast_to(tie8_np[:4, :4][None], (128, 4, 4)).copy(), name="tie4_c").ap()
    dr["tri8_c"] = nc.inline_tensor(
        np.fromfunction(lambda k, m: (k <= m).astype(np.float32), (8, 8)), name="tri8_c").ap()
    dr["i8_c"] = nc.inline_tensor(np.eye(8, dtype=np.float32), name="i8_c").ap()
    dr["mask8_c"] = nc.inline_tensor(
        np.fromfunction(lambda q, k: np.where(k <= q, 0.0, NEG).astype(np.float32),
                        (8, 8)), name="mask8_c").ap()

    base = (nc.sbuf_base + 31) // 32 * 32
    offsets, peak = layout_sbuf(base, nc.sbuf_top)
    T = {}
    for name, shape, dt, fp, lp in TENSORS:
        T[name] = nc.alloc_sbuf_tensor_at(name, shape, dt, offset=offsets[name]).ap()

    import contextlib
    with contextlib.ExitStack() as stack:
        psum = [
            stack.enter_context(nc.psum_tensor("ps_s0", [128, 512], F32)),
            stack.enter_context(nc.psum_tensor("ps_s1", [128, 512], F32)),
            stack.enter_context(nc.psum_tensor("ps_p0", [128, 512], F32)),
            stack.enter_context(nc.psum_tensor("ps_p1", [128, 512], F32)),
            stack.enter_context(nc.psum_tensor("ps_z0", [128, 1024], F32)),
            stack.enter_context(nc.psum_tensor("ps_z1", [128, 1024], F32)),
        ]
        psum = [p.ap() for p in psum]
        ctx = {"nc": nc, "T": T, "dr": dr, "psum": psum}

        kb = KB()
        phases = PHASES
        if DEBUG is not None:
            phases = PHASES[:DEBUG[0] + 1] + [make_debug_phase(DEBUG[1])]
        for fn in phases:
            for role in ROLES:
                fn(ctx, kb, role, E(kb, role, None))
        gate_h = stack.enter_context(nc.semaphore("gate"))
        for semname in sorted(kb.sems):
            kb.handles[semname] = stack.enter_context(nc.semaphore(semname))
        kb.reset_emit()

        with nc.Block() as blk:
            def mkbody(role):
                def body(eng):
                    for pi, fn in enumerate(phases):
                        fn(ctx, kb, role, E(kb, role, eng))
                        try:
                            eng.drain()
                        except Exception:
                            pass
                        eng.sem_inc(gate_h, 1)
                        eng.wait_ge(gate_h, 5 * (pi + 1))
                return body
            blk.sync(mkbody("sync"))
            blk.gpsimd(mkbody("gpsimd"))
            blk.vector(mkbody("vector"))
            blk.scalar(mkbody("scalar"))
            blk.tensor(mkbody("tensor"))

        for s2 in kb.sems:
            assert kb.sems[s2] == kb.emit_sems.get(s2, 0), \
                f"sem {s2}: plan {kb.sems[s2]} emit {kb.emit_sems.get(s2, 0)}"
    return nc



_NC_CACHE = []


def kernel(**inputs):
    x = np.ascontiguousarray(inputs["L0_attn_x"], dtype=np.float32)
    Wq, bq = inputs["L0_attn_Wq"], inputs["L0_attn_bq"]
    Wk, bk = inputs["L0_attn_Wk"], inputs["L0_attn_bk"]
    Wv, bv = inputs["L0_attn_Wv"], inputs["L0_attn_bv"]
    w_gate = inputs["L0_ffn_w_gate"]
    w1 = np.ascontiguousarray(inputs["L0_ffn_w1"], np.float32)
    w2 = np.ascontiguousarray(inputs["L0_ffn_w2"], np.float32)
    w3 = np.ascontiguousarray(inputs["L0_ffn_w3"], np.float32)
    sw1, sw2, sw3 = inputs["L0_ffn_sw1"], inputs["L0_ffn_sw2"], inputs["L0_ffn_sw3"]
    eb = np.asarray(inputs["L0_ffn_expert_bias"], np.float32)
    c = np.asarray(inputs["L0_c"], np.float32)
    wmod = np.ascontiguousarray(inputs["L0_wmod"], np.float32)

    def TT_(a):
        return np.ascontiguousarray(np.asarray(a, np.float32).T)

    shared = {
        "x8": np.ascontiguousarray(x[0, 0:8, :]),
        "cT": TT_(c),
        "wmod": wmod,
        "wqT": TT_(Wq), "wkT": TT_(Wk), "wvT": TT_(Wv),
        "bq_r": np.asarray(bq, np.float32).reshape(1, D),
        "bk_r": np.asarray(bk, np.float32).reshape(1, D),
        "bv_r": np.asarray(bv, np.float32).reshape(1, D),
        "bq8": np.ascontiguousarray(np.broadcast_to(np.asarray(bq, np.float32)[None, :], (8, D))),
        "bk8": np.ascontiguousarray(np.broadcast_to(np.asarray(bk, np.float32)[None, :], (8, D))),
        "bv8": np.ascontiguousarray(np.broadcast_to(np.asarray(bv, np.float32)[None, :], (8, D))),
        "sw1T": TT_(sw1), "sw2T": TT_(sw2), "sw3T": TT_(sw3),
        "wgT": TT_(w_gate),
        "eb_in": np.ascontiguousarray(np.broadcast_to(eb[None, :], (128, 8))),
        "w1_in": w1, "w3_in": w3, "w2_in": w2,
    }

    in_maps = []
    for core in range(8):
        seq, m = core // 4, core % 4
        slots = [4 * k + m for k in range(NSLOT)]
        x_q = np.concatenate([x[seq, ch * CH:(ch + 1) * CH, :] for ch in slots], axis=0)
        qv = np.empty((128, NTILE), np.float32)
        nv = np.empty((128, NTILE), np.float32)
        for sl2 in range(NSLOT):
            ch = slots[sl2]
            for t in range(4):
                tlc = sl2 * 4 + t
                idx = ch * CH + t * 128 + np.arange(128)
                qv[:, tlc] = idx
                nv[:, tlc] = seq * S + idx
        im = dict(shared)
        im["x_q"] = np.ascontiguousarray(x_q)
        im["x_kv"] = np.ascontiguousarray(x[seq])
        im["qidx"] = qv
        im["nidx"] = nv
        sel = np.zeros((128, 2), np.float32)
        sel[:, 0 if seq == 0 else 1] = 1.0
        im["sel_in"] = sel
        im["own_in"] = np.full((8, 1), 1.0 if core == 0 else 0.0, np.float32)
        in_maps.append(im)

    if not _NC_CACHE:
        _NC_CACHE.append(build_nc())
    nc = _NC_CACHE[0]
    res = run_bass_kernel_spmd(nc, in_maps, core_ids=list(range(8)))
    globals()['LAST_RES'] = res

    out = np.empty((2, S, D), np.float32)
    for core in range(8):
        seq, m = core // 4, core % 4
        yc = res.results[core]["y"]
        for k in range(NSLOT):
            ch = 4 * k + m
            out[seq, ch * CH:(ch + 1) * CH, :] = yc[k * CH:(k + 1) * CH, :]
    return out
